# version 02
# baseline (speedup 1.0000x reference)
import numpy as np
import concourse.bass as bass
import concourse.mybir as mybir
from concourse.alu_op_type import AluOpType
from concourse.bass_utils import run_bass_kernel_spmd
from concourse.aot_env import allow_ici_all_to_all

F32 = mybir.dt.float32
BF16 = mybir.dt.bfloat16
AF = mybir.ActivationFunctionType
OP = AluOpType

DBG = False
PARANOID = True
NCORE = 8
TOK = 16384
D = 1024
CI = 2048
NST = 16
DTR = 64
RTOK = 4096
ROUNDS = TOK // RTOK
SLAB = 512
UNIT = 1024
UPR = RTOK // UNIT
DCH = D // 128
MT = 4096 // 128
CT = CI // 128
EPS = 1e-5
WIN = SLAB + 3
XCOLS = ROUNDS * SLAB
XHALO = XCOLS + ROUNDS * 3


class Ctr:
    def __init__(self, sem):
        self.sem = sem
        self.n = 0


class Plan:
    ENGS = ("sync", "tensor", "vector", "scalar", "gpsimd")

    def __init__(self):
        self.streams = {e: [] for e in self.ENGS}
        self.seen = {e: {} for e in self.ENGS}
        self.own = {}

    def wait(self, eng, ctr, val=None):
        v = ctr.n if val is None else val
        if v <= 0:
            return
        if self.seen[eng].get(id(ctr), 0) >= v:
            return
        self.seen[eng][id(ctr)] = v
        self.streams[eng].append(("w", ctr, v))

    def op(self, eng, fn, inc, inck=1):
        own = self.own.get(eng)
        if own is not None and own.n > 0:
            self.wait(eng, own, own.n)
        if PARANOID:
            for c in self.ctrs.values():
                self.wait(eng, c, c.n)
        self.streams[eng].append(("o", fn, inc, inck))
        if inc is not None:
            inc.n += inck
            return inc.n
        return None

    def dma(self, eng, fn, inc):
        return self.op(eng, fn, inc, 16)

    def raw(self, eng, fn):
        self.streams[eng].append(("r", fn))

    def emit(self, eng_name, eng):
        for item in self.streams[eng_name]:
            if item[0] == "w":
                eng.wait_ge(item[1].sem, item[2])
            elif item[0] == "r":
                item[1](eng)
            else:
                _, fn, inc, inck = item
                ins = fn(eng)
                if inc is not None:
                    ins.then_inc(inc.sem, inck)


class Ring:
    def __init__(self, plan, nslots):
        self.P = plan
        self.n = nslots
        self.marks = [[] for _ in range(nslots)]
        self.i = 0

    def acquire(self, eng):
        s = self.i % self.n
        for (ctr, v) in self.marks[s]:
            self.P.wait(eng, ctr, v)
        self.marks[s] = []
        self.i += 1
        return s

    def mark(self, ctr, val=None):
        self.marks[(self.i - 1) % self.n].append(
            (ctr, ctr.n if val is None else val))


def build_program(nstage=99, fesub=99):
    nc = bass.Bass("TRN2", target_bir_lowering=False, debug=False)
    P = Plan()

    def din(name, shape):
        return nc.dram_tensor(name, shape, F32, kind="ExternalInput").ap()

    xT = din("xT", [D, XHALO])
    Win = din("Win", [D, 4096])
    convd = din("convd", [4 * CT, 128, 128])
    Wx = din("Wx", [CI, 96])
    WdtS = din("WdtS", [DTR, 256])
    Wout = din("Wout", [CI, D])
    WrT = din("WrT", [D, D])
    WkT = din("WkT", [D, D])
    WvT = din("WvT", [D, D])
    wmodS = din("wmodS", [D, 768])
    KC = din("KC", [128, 224])
    O_CV, O_MUR, O_MUK, O_CB, O_DV, O_BDT, O_AL, O_ONE, O_MSK, O_MSKH, O_ID = \
        0, 8, 16, 24, 40, 56, 58, 90, 91, 92, 93
    O_EPS, O_ONEF, O_NHALF = 221, 222, 223

    outT = nc.dram_tensor("outT", [D, XCOLS], F32, kind="ExternalOutput").ap()
    dbg = {}
    if DBG:
        dbg["h"] = nc.dram_tensor("dbg_h", [128, DCH * WIN], BF16, kind="ExternalOutput").ap()
        dbg["xm"] = nc.dram_tensor("dbg_xm", [128, DCH * SLAB], F32, kind="ExternalOutput").ap()
        dbg["l2"] = nc.dram_tensor("dbg_l2", [128, DCH * (SLAB + 1)], BF16, kind="ExternalOutput").ap()
        dbg["rr"] = nc.dram_tensor("dbg_rr", [128, DCH * SLAB], BF16, kind="ExternalOutput").ap()
        dbg["ka"] = nc.dram_tensor("dbg_ka", [128, DCH * SLAB], BF16, kind="ExternalOutput").ap()
        dbg["dl"] = nc.dram_tensor("dbg_dl", [128, 2 * UNIT], F32, kind="ExternalOutput").ap()
        dbg["yev"] = nc.dram_tensor("dbg_yev", [128, 2 * UNIT], BF16, kind="ExternalOutput").ap()
        dbg["yt"] = nc.dram_tensor("dbg_yt", [2, 128, DCH * SLAB], BF16, kind="ExternalOutput").ap()
        dbg["mod"] = nc.dram_tensor("dbg_mod", [128, 48], F32, kind="ExternalOutput").ap()
        dbg["ybe"] = nc.dram_tensor("dbg_ybe", [128, 2 * SLAB], BF16, kind="ExternalOutput").ap()
        dbg["row"] = nc.dram_tensor("dbg_row", [1, 2080], F32, kind="ExternalOutput").ap()
        dbg["agmo"] = nc.dram_tensor("dbg_agmo", [6144], F32, kind="ExternalOutput").ap()
        dbg["bc"] = nc.dram_tensor("dbg_bc", [128, 40], BF16, kind="ExternalOutput").ap()
        dbg["spr"] = nc.dram_tensor("dbg_spr", [128, 24], F32, kind="ExternalOutput").ap()
        dbg["xc"] = nc.dram_tensor("dbg_xc", [128, CT * WIN], BF16, kind="ExternalOutput").ap()

    winS = nc.dram_tensor("winS", [DCH, 128, 4096], BF16)
    woutS = nc.dram_tensor("woutS", [CT, 128, D], BF16)
    wrS = nc.dram_tensor("wrS", [DCH, 128, D], BF16)
    wkS = nc.dram_tensor("wkS", [DCH, 128, D], BF16)
    wvS = nc.dram_tensor("wvS", [DCH, 128, D], BF16)
    vecS = nc.dram_tensor("vecS", [8, 4096], F32)
    rowS = nc.dram_tensor("rowS", [4, WIN], F32)
    xc_st = nc.dram_tensor("xc_st", [ROUNDS, CT, 128, SLAB], BF16)
    zp_st = nc.dram_tensor("zp_st", [ROUNDS, CT, 128, SLAB], BF16)
    xcAi = [nc.dram_tensor(f"xcAi{k}", [NCORE, 2, 128, SLAB], BF16) for k in range(ROUNDS)]
    xcAo = [nc.dram_tensor(f"xcAo{k}", [NCORE, 2, 128, SLAB], BF16) for k in range(ROUNDS)]
    agxi = [nc.dram_tensor(f"agxi{k}", [96, SLAB], BF16) for k in range(ROUNDS)]
    agxo = [nc.dram_tensor(f"agxo{k}", [NCORE, 96, SLAB], BF16) for k in range(ROUNDS)]
    yAi = [nc.dram_tensor(f"yAi{k}", [NCORE, 2, 128, SLAB], BF16) for k in range(ROUNDS)]
    yAo = [nc.dram_tensor(f"yAo{k}", [NCORE, 2, 128, SLAB], BF16) for k in range(ROUNDS)]
    agli = [nc.dram_tensor(f"agli{k}", [D], BF16) for k in range(ROUNDS)]
    aglo = [nc.dram_tensor(f"aglo{k}", [NCORE, D], BF16) for k in range(ROUNDS)]
    agmi = nc.dram_tensor("agmi", [768], F32)
    agmo = nc.dram_tensor("agmo", [6144], F32)

    ctx = []

    def sb(name, pr, cols, dt):
        t = nc.sbuf_tensor(name, [pr, cols], dt)
        ctx.append(t)
        return t.__enter__()

    def psm(name, cols):
        t = nc.psum_tensor(name, [128, cols], F32)
        ctx.append(t)
        return t.__enter__()

    def sem(name):
        t = nc.semaphore(name)
        ctx.append(t)
        return Ctr(t.__enter__())

    s_kc = sb("s_kc", 128, 224, F32)
    s_asc = sb("s_asc", 128, 32, F32)
    s_mod = sb("s_mod", 128, 48, F32)
    M_SA, M_CA1, M_GA, M_SB, M_CB1, M_GB = 0, 8, 16, 24, 32, 40
    s_mu1 = sb("s_mu1", 128, 16, F32)
    s_bxz = sb("s_bxz", 128, MT, F32)
    s_br = sb("s_br", 128, DCH, F32)
    s_bk = sb("s_bk", 128, DCH, F32)
    s_cr = sb("s_cr", 128, DCH, F32)
    s_ck = sb("s_ck", 128, DCH, F32)
    s_bc = sb("s_bc", 128, 40, BF16)
    B_SIL, B_SA, B_SB, B_MBR, B_MBK = 0, 8, 16, 24, 32

    s_diag = sb("s_diag", 128, 4 * CT * 128, BF16)
    s_wx = sb("s_wx", 128, CT * 96, BF16)
    s_wdt = sb("s_wdt", DTR, 256, BF16)
    s_id = sb("s_id", 128, 128, BF16)
    s_one = sb("s_one", 128, 1, BF16)
    s_wpF = sb("s_wpF", 128, 2 * 512, BF16)
    s_wpB = sb("s_wpB", 128, 2 * 512, BF16)

    s_xw = sb("s_xw", 128, 2 * WIN, F32)
    s_xb = sb("s_xb", 128, DCH * WIN, BF16)
    s_sq = sb("s_sq", 128, 2 * WIN, BF16)
    s_row = sb("s_row", 1, 2080, F32)
    R_M, R_S, R_V, R_I = 0, 520, 1040, 1560
    s_mrep = sb("s_mrep", 128, WIN, F32)
    s_irep = sb("s_irep", 128, WIN, F32)
    s_h = sb("s_h", 128, DCH * WIN, BF16)
    s_xe = sb("s_xe", 128, CT * WIN, BF16)
    s_zx = sb("s_zx", 128, 4 * SLAB, BF16)
    s_xdb = sb("s_xdb", 96, SLAB, BF16)

    s_dr = sb("s_dr", DTR, 2 * UNIT, BF16)
    s_dl = sb("s_dl", 128, 2 * UNIT, F32)
    s_dlb = sb("s_dlb", 128, 2 * UNIT, F32)
    s_xcu = sb("s_xcu", 128, 2 * UNIT, BF16)
    s_u = sb("s_u", 128, 2 * UNIT, BF16)
    s_brp = sb("s_brp", 128, 2 * UNIT, BF16)
    s_crp = sb("s_crp", 128, 2 * UNIT, BF16)
    s_da = sb("s_da", 128, 2 * UNIT, F32)
    s_d2 = sb("s_d2", 128, 2 * UNIT, BF16)
    s_hs = sb("s_hs", 128, 4 * UNIT, BF16)
    s_tmp = sb("s_tmp", 128, 4 * UNIT, BF16)
    s_car = sb("s_car", 128, 2 * NST, F32)
    s_yev = sb("s_yev", 128, 2 * UNIT, BF16)

    s_ybe = sb("s_ybe", 128, 2 * SLAB, BF16)
    s_xcr = sb("s_xcr", 128, 2 * SLAB, BF16)
    s_szr = sb("s_szr", 128, 2 * SLAB, BF16)
    s_x2 = sb("s_x2", 128, 2 * SLAB, F32)
    s_xm = sb("s_xm", 128, DCH * SLAB, F32)
    s_l2 = sb("s_l2", 128, DCH * (SLAB + 1), BF16)
    s_rr = sb("s_rr", 128, DCH * SLAB, BF16)
    s_ka = sb("s_ka", 128, DCH * SLAB, BF16)
    s_t1 = sb("s_t1", 128, 2 * SLAB, F32)
    s_lc = sb("s_lc", 128, DCH, BF16)
    s_silc = sb("s_silc", 128, DCH, F32)
    s_tr = sb("s_tr", 48, 128, F32)
    s_pr = sb("s_pr", 128, 1024, F32)

    psA = psm("psA", 512)
    psB = psm("psB", 512)
    psC = psm("psC", 512)
    psD = psm("psD", 512)
    psY0 = psm("psY0", UNIT)
    psY1 = psm("psY1", UNIT)

    nc.const_aps.register(F32, EPS, s_kc[:, 221:222])
    nc.const_aps.register(F32, 1.0, s_kc[:, 222:223])
    nc.const_aps.register(F32, -0.5, s_kc[:, 223:224])

    qS = sem("qS")
    qG = sem("qG")
    cP = sem("cP")
    cA = sem("cA")
    cV = sem("cV")
    cG = sem("cG")
    cc = sem("cc")
    P.ctrs = dict(qS=qS, qG=qG, cP=cP, cA=cA, cV=cV, cG=cG, cc=cc)
    P.own = {"vector": cV, "scalar": cA, "gpsimd": cG}

    def dmaS(dst, src):
        return P.dma("sync", lambda e, d=dst, s=src: e.dma_start(out=d, in_=s), qS)

    def dmaG(dst, src):
        return P.dma("gpsimd", lambda e, d=dst, s=src: e.dma_start(out=d, in_=s), qG)

    def mm(out, lhsT, rhs, start, stop):
        return P.op("tensor", lambda e, o=out, l=lhsT, r=rhs, s1=start, s2=stop:
                    e.matmul(o, l, r, start=s1, stop=s2), cP)

    def act(out, in_, func, bias=None, scale=None):
        kw = {}
        if bias is not None:
            kw["bias"] = bias
        if scale is not None:
            kw["scale"] = scale
        return P.op("scalar", lambda e, o=out, i=in_, f=func, kw=kw:
                    e.activation(out=o, in_=i, func=f, **kw), cA)

    def vtt(out, a, b, op=OP.mult):
        return P.op("vector", lambda e, o=out, x=a, y=b, p=op:
                    e.tensor_tensor(out=o, in0=x, in1=y, op=p), cV)

    def gtt(out, a, b, op=OP.mult):
        return P.op("gpsimd", lambda e, o=out, x=a, y=b, p=op:
                    e.tensor_tensor(out=o, in0=x, in1=y, op=p), cG)

    def vts(out, in_, s1, s2=None, op0=OP.mult, op1=None):
        def f(e, o=out, i=in_, s1=s1, s2=s2, op0=op0, op1=op1):
            if op1 is None:
                return e.tensor_scalar(out=o, in0=i, scalar1=s1, scalar2=None, op0=op0)
            return e.tensor_scalar(out=o, in0=i, scalar1=s1, scalar2=s2, op0=op0, op1=op1)
        return P.op("vector", f, cV)

    def vstt(out, in0, scalar, in1, op0, op1):
        return P.op("vector", lambda e, o=out, a=in0, s=scalar, b=in1, p0=op0, p1=op1:
                    e.scalar_tensor_tensor(out=o, in0=a, scalar=s, in1=b, op0=p0, op1=p1), cV)

    def gstt(out, in0, scalar, in1, op0, op1):
        return P.op("gpsimd", lambda e, o=out, a=in0, s=scalar, b=in1, p0=op0, p1=op1:
                    e.scalar_tensor_tensor(out=o, in0=a, scalar=s, in1=b, op0=p0, op1=p1), cG)

    def vcp(out, in_):
        return P.op("vector", lambda e, o=out, i=in_: e.tensor_copy(o, i), cV)

    def gcp(out, in_):
        return P.op("gpsimd", lambda e, o=out, i=in_: e.tensor_copy(o, i), cG)

    def acp(out, in_):
        return P.op("scalar", lambda e, o=out, i=in_: e.copy(out=o, in_=i), cA)

    def ldw(w):
        P.op("tensor", lambda e, x=w: e.ldweights(x), None)

    def coll(kind, ins, outs):
        def f(e, k=kind, i=ins, o=outs):
            with allow_ici_all_to_all():
                return e.collective_compute(
                    k, OP.bypass, replica_groups=[list(range(NCORE))],
                    ins=[i], outs=[o])
        return P.op("gpsimd", f, cc)

    def cols_from_hbm(src_flat, ncols, dst_ap):
        P.wait("sync", cP)
        dmaS(s_tr[0:ncols, 0:128], src_flat.rearrange("(a b) -> a b", b=128))
        P.wait("tensor", qS)
        P.op("tensor", lambda e, n=ncols: e.transpose(
            psD[:, 0:n], s_tr[0:n, 0:128], s_kc[0:n, O_ID:O_ID + n]), cP)
        P.wait("scalar", cP)
        acp(dst_ap, psD[:, 0:ncols])

    dmaS(s_kc[:, :], KC)
    P.wait("vector", qS)
    P.wait("scalar", qS)
    vcp(s_one[:], s_kc[:, O_ONE:O_ONE + 1])
    vcp(s_id[:], s_kc[:, O_ID:O_ID + 128])
    act(s_asc[:], s_kc[:, O_AL:O_AL + 32], AF.Exp)
    P.wait("vector", cA)
    vts(s_asc[:], s_asc[:], -1.0)
    vts(s_mu1[:, 0:DCH], s_kc[:, O_MUR:O_MUR + DCH], -1.0, 1.0, OP.mult, OP.add)
    vts(s_mu1[:, 8:8 + DCH], s_kc[:, O_MUK:O_MUK + DCH], -1.0, 1.0, OP.mult, OP.add)

    def conv_copy(src_ap, dst_ap, tmpcols):
        P.wait("sync", cV)
        dmaS(s_pr[:, 0:tmpcols], src_ap)
        P.wait("vector", qS)
        vcp(dst_ap, s_pr[:, 0:tmpcols])

    for ct in range(CT):
        conv_copy(Wx[ct * 128:(ct + 1) * 128, :], s_wx[:, ct * 96:(ct + 1) * 96], 96)
    P.wait("sync", cV)
    dmaS(s_pr[0:DTR, 0:256], WdtS)
    P.wait("vector", qS)
    vcp(s_wdt[:], s_pr[0:DTR, 0:256])
    for i in range(4 * CT):
        conv_copy(convd[i], s_diag[:, i * 128:(i + 1) * 128], 128)

    def to_scratch(src, scr, nchunk, ccols):
        for j in range(nchunk):
            for p in range(0, ccols, 512):
                P.wait("sync", cV)
                dmaS(s_pr[:, 0:512], src[j * 128:(j + 1) * 128, p:p + 512])
                P.wait("vector", qS)
                P.wait("vector", qG)
                vcp(s_wpF[:, 0:512], s_pr[:, 0:512])
                P.wait("gpsimd", cV)
                dmaG(scr.ap()[j][:, p:p + 512], s_wpF[:, 0:512])

    to_scratch(Win, winS, DCH, 4096)
    to_scratch(Wout, woutS, CT, D)
    to_scratch(WrT, wrS, DCH, D)
    to_scratch(WkT, wkS, DCH, D)
    to_scratch(WvT, wvS, DCH, D)
    scr_done = qG.n

    P.wait("scalar", cV)
    P.wait("scalar", qG)
    act(s_silc[:, :], s_kc[:, O_CV:O_CV + DCH], AF.Silu)
    silc = cA.n
    for piece in range(2):
        for j in range(DCH):
            P.wait("sync", cP)
            dmaS(s_pr[:, 512:512 + 384],
                 wmodS[j * 128:(j + 1) * 128, piece * 384:(piece + 1) * 384])
            P.wait("tensor", qS)
            P.wait("tensor", cA, silc)
            mm(psA[0:1, 0:384], s_silc[:, j:j + 1],
               s_pr[:, 512:512 + 384], j == 0, j == DCH - 1)
        P.wait("scalar", cP)
        acp(s_row[0:1, piece * 384:piece * 384 + 384], psA[0:1, 0:384])
        P.wait("tensor", cA)
    P.wait("gpsimd", cA)
    dmaG(agmi.ap().unsqueeze(0), s_row[0:1, 0:768])
    P.wait("gpsimd", qG)
    coll("AllGather", agmi.ap().opt(), agmo.ap().opt())
    P.wait("sync", cc)
    cols_from_hbm(agmo.ap(), 48, s_mod[:, 0:48])
    P.wait("vector", cA)
    P.wait("scalar", cA)
    vts(s_mod[:, M_CA1:M_CA1 + DCH], s_mod[:, M_CA1:M_CA1 + DCH], 1.0, op0=OP.add)
    vts(s_mod[:, M_CB1:M_CB1 + DCH], s_mod[:, M_CB1:M_CB1 + DCH], 1.0, op0=OP.add)
    for col in (M_GA, M_GB):
        vts(s_pr[:, 0:DCH], s_mod[:, col:col + DCH], 0.0, op0=OP.is_ge)
        vts(s_pr[:, 0:DCH], s_pr[:, 0:DCH], 2.0, op0=OP.mult)
        vts(s_pr[:, 0:DCH], s_pr[:, 0:DCH], -1.0, op0=OP.add)
        P.wait("scalar", cV)
        act(s_pr[:, 8:8 + DCH], s_mod[:, col:col + DCH], AF.Abs)
        P.wait("vector", cA)
        vts(s_pr[:, 8:8 + DCH], s_pr[:, 8:8 + DCH], 0.25, op0=OP.max)
        vtt(s_mod[:, col:col + DCH], s_pr[:, 0:DCH], s_pr[:, 8:8 + DCH])
        if DBG and col == M_GA:
            P.wait("gpsimd", cV)
            dmaG(dbg["spr"], s_pr[:, 0:24])

    vcp(s_bc[:, B_SA:B_SA + DCH], s_mod[:, M_SA:M_SA + DCH])
    vcp(s_bc[:, B_SB:B_SB + DCH], s_mod[:, M_SB:M_SB + DCH])
    vtt(s_pr[:, 16:16 + DCH], s_mu1[:, 0:DCH], s_mod[:, M_SB:M_SB + DCH])
    vcp(s_bc[:, B_MBR:B_MBR + DCH], s_pr[:, 16:16 + DCH])
    vtt(s_pr[:, 16:16 + DCH], s_mu1[:, 8:8 + DCH], s_mod[:, M_SB:M_SB + DCH])
    vcp(s_bc[:, B_MBK:B_MBK + DCH], s_pr[:, 16:16 + DCH])
    cols_rdy = cV.n

    def matvec_scratch(scol, scr, kchunks, outcols, dst, slot):
        for mp in range(outcols // 512):
            for j in range(kchunks):
                P.wait("sync", cP)
                dmaS(s_wpB[:, 0:512], scr.ap()[j][:, mp * 512:(mp + 1) * 512])
                P.wait("tensor", qS)
                P.wait("tensor", cV, cols_rdy)
                ldw(s_bc[:, scol + j:scol + j + 1])
                mm(psB[0:1, :], s_bc[:, scol + j:scol + j + 1], s_wpB[:, 0:512],
                   j == 0, j == kchunks - 1)
            P.wait("scalar", cP)
            P.wait("scalar", qG)
            acp(s_row[0:1, 1024:1024 + 512], psB[0:1, :])
            P.wait("tensor", cA)
            P.wait("gpsimd", cA)
            dmaG(vecS.ap()[slot][mp * 512:(mp + 1) * 512].unsqueeze(0),
                 s_row[0:1, 1024:1024 + 512])
        P.wait("sync", qG)
        cols_from_hbm(vecS.ap()[slot][0:outcols], outcols // 128, dst)

    P.wait("sync", qG, scr_done)
    matvec_scratch(B_SA, winS, DCH, 4096, s_bxz[:], 0)
    matvec_scratch(B_SB, wrS, DCH, D, s_br[:], 1)
    matvec_scratch(B_SB, wkS, DCH, D, s_bk[:], 2)
    matvec_scratch(B_MBR, wrS, DCH, D, s_cr[:], 3)
    matvec_scratch(B_MBK, wkS, DCH, D, s_ck[:], 4)
    P.wait("vector", cA)
    for t in (s_cr, s_ck):
        vts(t[:], t[:], -1.0)
        P.op("vector", lambda e, t=t: e.tensor_scalar(
            out=t[:], in0=t[:], scalar1=s_kc[:, O_MSK:O_MSK + 1], scalar2=None,
            op0=OP.mult), cV)

    if DBG:
        P.wait("gpsimd", cV)
        P.wait("gpsimd", cA)
        dmaG(dbg["mod"], s_mod[:, :])
        dmaG(dbg["row"], s_row[0:1, :])
        dmaG(dbg["agmo"], agmo.ap())
        dmaG(dbg["bc"], s_bc[:, :])

    rg_wpF = Ring(P, 2)
    rg_wpB = Ring(P, 2)
    rg_xw = Ring(P, 2)
    rg_sq = Ring(P, 2)
    rg_zx = Ring(P, 4)
    rg_dr = Ring(P, 2)
    rg_dl = [Ring(P, 2), Ring(P, 2)]
    rg_xcu = [Ring(P, 1), Ring(P, 1)]
    rg_u = [Ring(P, 1), Ring(P, 1)]
    rg_bc = Ring(P, 2)
    rg_da = Ring(P, 2)
    rg_hs = Ring(P, 4)
    rg_tmp = Ring(P, 4)
    rg_yev = Ring(P, 1)
    rg_be = Ring(P, 2)
    rg_t1 = Ring(P, 2)
    rg_x2 = Ring(P, 2)

    cc_agx = [0] * ROUNDS
    cc_y = [0] * ROUNDS
    cc_agl = [0] * ROUNDS
    last_tts = [0]

    def front_end(k, fesub=99):
        t0 = k * SLAB
        P.wait("gpsimd", cP)
        P.wait("gpsimd", cV)
        P.wait("sync", cV)
        P.wait("sync", cA)

        for j in range(DCH):
            sl = rg_xw.acquire("sync")
            xwv = s_xw[:, sl * WIN:(sl + 1) * WIN]
            dmaS(xwv[:, 3:WIN], xT[j * 128:(j + 1) * 128, t0:t0 + SLAB])
            dmaS(xwv[:, 0:3],
                 xT[j * 128:(j + 1) * 128, XCOLS + k * 3:XCOLS + (k + 1) * 3])
            P.wait("gpsimd", qS)
            xbv = s_xb[:, j * WIN:(j + 1) * WIN]
            gcp(xbv, xwv)
            rg_xw.mark(cG)
            sq = rg_sq.acquire("gpsimd")
            sqv = s_sq[:, sq * WIN:(sq + 1) * WIN]
            gtt(sqv, xwv, xwv)
            rg_xw.mark(cG)
            P.wait("tensor", cG)
            ldw(s_one[:])
            mm(psD[0:1, 0:512], s_one[:], xbv[:, 3:WIN], j == 0, j == DCH - 1)
            mm(psB[0:1, 0:3], s_one[:], xbv[:, 0:3], j == 0, j == DCH - 1)
            mm(psC[0:1, 0:512], s_one[:], sqv[:, 3:WIN], j == 0, j == DCH - 1)
            mm(psB[0:1, 8:11], s_one[:], sqv[:, 0:3], j == 0, j == DCH - 1)
            rg_sq.mark(cP)
        P.wait("vector", cP)
        vts(s_row[0:1, R_M + 3:R_M + WIN], psD[0:1, 0:512], 1.0 / D)
        vts(s_row[0:1, R_M:R_M + 3], psB[0:1, 0:3], 1.0 / D)
        vts(s_row[0:1, R_S + 3:R_S + WIN], psC[0:1, 0:512], 1.0 / D)
        vts(s_row[0:1, R_S:R_S + 3], psB[0:1, 8:11], 1.0 / D)
        vtt(s_row[0:1, R_V:R_V + WIN], s_row[0:1, R_M:R_M + WIN],
            s_row[0:1, R_M:R_M + WIN])
        vtt(s_row[0:1, R_V:R_V + WIN], s_row[0:1, R_S:R_S + WIN],
            s_row[0:1, R_V:R_V + WIN], op=OP.subtract)
        P.wait("scalar", cV)
        act(s_row[0:1, R_I:R_I + WIN], s_row[0:1, R_V:R_V + WIN], AF.Ln, bias=EPS)
        act(s_row[0:1, R_I:R_I + WIN], s_row[0:1, R_I:R_I + WIN], AF.Exp, scale=-0.5)
        P.wait("gpsimd", cA)
        dmaG(rowS.ap()[0].unsqueeze(0), s_row[0:1, R_M:R_M + WIN])
        dmaG(rowS.ap()[1].unsqueeze(0), s_row[0:1, R_I:R_I + WIN])
        P.wait("sync", qG)
        dmaS(s_mrep[:], rowS.ap()[0].unsqueeze(0).broadcast_to((128, WIN)))
        dmaS(s_irep[:], rowS.ap()[1].unsqueeze(0).broadcast_to((128, WIN)))
        P.wait("vector", qS)
        for j in range(DCH):
            xbv = s_xb[:, j * WIN:(j + 1) * WIN]
            vtt(s_t1[:, 0:WIN], xbv, s_mrep[:], op=OP.subtract)
            vstt(s_h[:, j * WIN:(j + 1) * WIN], s_t1[:, 0:WIN],
                 s_mod[:, M_CA1 + j:M_CA1 + j + 1], s_irep[:], OP.mult, OP.mult)
        h_rdy = cV.n
        if DBG and k == 0:
            P.wait("gpsimd", cV, h_rdy)
            dmaG(dbg["h"], s_h[:, :])
        if fesub <= 1:
            return

        P.wait("sync", qG, scr_done)
        pslist = (psA, psB, psC)
        zi_ring = {}
        evac_prev = [0]
        for g in range(NG_WIN):
            mts = list(range(g * 3, min(MT, (g + 1) * 3)))
            w = len(mts) * 128
            for j in range(DCH):
                sl = rg_wpF.acquire("sync")
                dmaS(s_wpF[:, sl * 512:sl * 512 + w],
                     winS.ap()[j][:, mts[0] * 128:mts[0] * 128 + w])
                P.wait("tensor", qS)
                P.wait("tensor", cV, h_rdy)
                P.wait("tensor", cA, evac_prev[0])
                for mi, mt_ in enumerate(mts):
                    wp = s_wpF[:, sl * 512 + mi * 128:sl * 512 + (mi + 1) * 128]
                    ldw(wp)
                    mm(pslist[mi][:, :], wp, s_h[:, j * WIN + 3:j * WIN + WIN],
                       j == 0, j == DCH - 1)
                    if fesub >= 12 and mts[0] < CT:
                        hps = (psD, psY0, psY1)[mi]
                        mm(hps[:, 0:3], wp,
                           s_h[:, j * WIN:j * WIN + 3], j == 0, j == DCH - 1)
                rg_wpF.mark(cP)
            P.wait("scalar", cP)
            for mi, mt_ in enumerate(mts):
                if mt_ < CT:
                    if fesub >= 13:
                        act(s_xe[:, mt_ * WIN + 3:mt_ * WIN + WIN], pslist[mi][:, :],
                            AF.Identity, bias=s_bxz[:, mt_:mt_ + 1])
                    if fesub >= 14:
                        hps = (psD, psY0, psY1)[mi]
                        act(s_xe[:, mt_ * WIN:mt_ * WIN + 3], hps[:, 0:3],
                            AF.Identity, bias=s_bxz[:, mt_:mt_ + 1])
                else:
                    if fesub >= 15:
                        zi = mt_ - CT
                        zr = rg_zx.acquire("scalar")
                        act(s_zx[:, zr * SLAB:(zr + 1) * SLAB], pslist[mi][:, :],
                            AF.Identity, bias=s_bxz[:, mt_:mt_ + 1])
                        P.wait("gpsimd", cA)
                        dmaG(zp_st.ap()[k, zi], s_zx[:, zr * SLAB:(zr + 1) * SLAB])
                        rg_zx.mark(qG)
            evac_prev[0] = cA.n
        if fesub <= 19:
            return
        if fesub <= 2:
            return
        if k == 0:
            P.wait("vector", cA)
            for ct in range(CT):
                P.op("vector", lambda e, ct=ct: e.tensor_scalar(
                    out=s_xe[:, ct * WIN:ct * WIN + 3],
                    in0=s_xe[:, ct * WIN:ct * WIN + 3],
                    scalar1=s_kc[:, O_MSKH:O_MSKH + 1], scalar2=None,
                    op0=OP.mult), cV)

        for ct in range(CT):
            P.wait("tensor", cA)
            if k == 0:
                P.wait("tensor", cV)
            for kk in range(4):
                dg = s_diag[:, (kk * CT + ct) * 128:(kk * CT + ct + 1) * 128]
                ldw(dg)
                mm(psD[:, :], dg, s_xe[:, ct * WIN + kk:ct * WIN + kk + SLAB],
                   kk == 0, kk == 3)
            P.wait("scalar", cP)
            act(s_xe[:, ct * WIN + 3:(ct + 1) * WIN], psD[:, :], AF.Identity,
                bias=s_kc[:, O_CB + ct:O_CB + ct + 1])
        if fesub <= 3:
            return
        for ct in range(CT):
            act(s_xe[:, ct * WIN + 3:(ct + 1) * WIN],
                s_xe[:, ct * WIN + 3:(ct + 1) * WIN], AF.Silu)
        if DBG and k == 0:
            P.wait("gpsimd", cA)
            dmaG(dbg["xc"], s_xe[:, :])
        P.wait("tensor", cA)
        for ct in range(CT):
            ldw(s_wx[:, ct * 96:(ct + 1) * 96])
            mm(psD[0:96, :], s_wx[:, ct * 96:(ct + 1) * 96],
               s_xe[:, ct * WIN + 3:(ct + 1) * WIN], ct == 0, ct == CT - 1)
        P.wait("scalar", cP)
        act(s_xdb[:, :], psD[0:96, :], AF.Copy)
        if fesub <= 4:
            return
        P.wait("gpsimd", cA)
        for ct in range(CT):
            dmaG(xcAi[k].ap()[ct // 2, ct % 2], s_xe[:, ct * WIN + 3:(ct + 1) * WIN])
            dmaG(xc_st.ap()[k, ct], s_xe[:, ct * WIN + 3:(ct + 1) * WIN])
        dmaG(agxi[k].ap(), s_xdb[:, :])
        if fesub <= 5:
            return
        P.wait("gpsimd", qG)
        coll("AllToAll", xcAi[k].ap().opt(), xcAo[k].ap().opt())
        coll("AllGather", agxi[k].ap().opt(), agxo[k].ap().opt())
        cc_agx[k] = cc.n

    NG_WIN = (MT + 2) // 3

    def scan(k):
        P.wait("sync", cc, cc_agx[k])
        P.wait("sync", cV)
        P.wait("sync", cA)
        yev_prev = [0]
        for u in range(UPR):
            sl = rg_dr.acquire("sync")
            dmaS(s_dr[:, sl * UNIT:sl * UNIT + 512], agxo[k].ap()[2 * u, 0:DTR, :])
            dmaS(s_dr[:, sl * UNIT + 512:(sl + 1) * UNIT],
                 agxo[k].ap()[2 * u + 1, 0:DTR, :])
            for cti in range(2):
                rg_xcu[cti].acquire("sync")
                dmaS(s_xcu[:, cti * UNIT:cti * UNIT + 512], xcAo[k].ap()[2 * u, cti])
                dmaS(s_xcu[:, cti * UNIT + 512:(cti + 1) * UNIT],
                     xcAo[k].ap()[2 * u + 1, cti])
            dru = qS.n
            P.wait("tensor", qS, dru)
            P.wait("tensor", cA)
            for cti in range(2):
                dlr = rg_dl[cti].acquire("scalar")
                dlv = (s_dl if cti == 0 else s_dlb)
                P.wait("tensor", cA)
                for hf in range(2):
                    ldw(s_wdt[:, cti * 128:(cti + 1) * 128])
                    mm((psA if hf == 0 else psB)[:, :],
                       s_wdt[:, cti * 128:(cti + 1) * 128],
                       s_dr[:, sl * UNIT + hf * 512:sl * UNIT + (hf + 1) * 512],
                       True, True)
                rg_dr.mark(cP)
                P.wait("scalar", cP)
                P.wait("scalar", cV, last_tts[0])
                for hf in range(2):
                    pp = psA if hf == 0 else psB
                    act(s_da[:, hf * 512:(hf + 1) * 512] if cti == 0 else
                        s_da[:, UNIT + hf * 512:UNIT + (hf + 1) * 512],
                        pp[:, :], AF.Exp, bias=s_kc[:, O_BDT + cti:O_BDT + cti + 1])
                act(dlv[:, dlr * UNIT:(dlr + 1) * UNIT],
                    s_da[:, 0:UNIT] if cti == 0 else s_da[:, UNIT:2 * UNIT],
                    AF.Ln, bias=1.0)
                P.wait("vector", cA)
                P.wait("vector", qS, dru)
                rg_u[cti].acquire("vector")
                vtt(s_u[:, cti * UNIT:(cti + 1) * UNIT],
                    dlv[:, dlr * UNIT:(dlr + 1) * UNIT],
                    s_xcu[:, cti * UNIT:(cti + 1) * UNIT])
                rg_xcu[cti].mark(cV)
                rg_dl[cti].mark(cV)
            uvn = cV.n

            for n in range(NST):
                bs = rg_bc.acquire("sync")
                for hf in range(2):
                    src = agxo[k].ap()[2 * u + hf]
                    dmaS(s_brp[:, bs * UNIT + hf * 512:bs * UNIT + (hf + 1) * 512],
                         src[64 + n].unsqueeze(0).broadcast_to((128, 512)))
                    dmaS(s_crp[:, bs * UNIT + hf * 512:bs * UNIT + (hf + 1) * 512],
                         src[80 + n].unsqueeze(0).broadcast_to((128, 512)))
                bcn = qS.n
                for cti in range(2):
                    dlr = (rg_dl[cti].i - 1) % 2
                    dlv = (s_dl if cti == 0 else s_dlb)
                    da = rg_da.acquire("scalar")
                    act(s_da[:, da * UNIT:(da + 1) * UNIT],
                        dlv[:, dlr * UNIT:(dlr + 1) * UNIT], AF.Exp,
                        scale=s_asc[:, cti * NST + n:cti * NST + n + 1])
                    rg_dl[cti].mark(cA)
                    da_n = cA.n
                    P.wait("vector", qS, bcn)
                    P.wait("vector", cV, uvn)
                    vtt(s_d2[:, da * UNIT:(da + 1) * UNIT],
                        s_u[:, cti * UNIT:(cti + 1) * UNIT],
                        s_brp[:, bs * UNIT:(bs + 1) * UNIT])
                    rg_u[cti].mark(cV)
                    rg_bc.mark(cV)
                    hsl = rg_hs.acquire("vector")
                    P.wait("vector", cA, da_n)
                    first_u = (k == 0 and u == 0)
                    init = 0.0 if first_u else s_car[:, cti * NST + n:cti * NST + n + 1]
                    P.op("vector", lambda e, hsl=hsl, da=da, init=init:
                         e.tensor_tensor_scan(
                             out=s_hs[:, hsl * UNIT:(hsl + 1) * UNIT],
                             data0=s_da[:, da * UNIT:(da + 1) * UNIT],
                             data1=s_d2[:, da * UNIT:(da + 1) * UNIT],
                             initial=init, op0=OP.mult, op1=OP.add), cV)
                    rg_da.mark(cV)
                    tts_n = cV.n
                    P.wait("scalar", cV, tts_n)
                    acp(s_car[:, cti * NST + n:cti * NST + n + 1],
                        s_hs[:, (hsl + 1) * UNIT - 1:(hsl + 1) * UNIT])
                    rg_hs.mark(cA)
                    tsl = rg_tmp.acquire("gpsimd")
                    P.wait("gpsimd", cV, tts_n)
                    gtt(s_tmp[:, tsl * UNIT:(tsl + 1) * UNIT],
                        s_hs[:, hsl * UNIT:(hsl + 1) * UNIT],
                        s_crp[:, bs * UNIT:(bs + 1) * UNIT])
                    rg_hs.mark(cG)
                    rg_bc.mark(cG)
                    ym_n = cG.n
                    P.wait("tensor", cG, ym_n)
                    if n == 0:
                        P.wait("tensor", cA, yev_prev[0])
                    psY = psY0 if cti == 0 else psY1
                    ldw(s_id[:])
                    mm(psY[:, 0:512], s_id[:],
                       s_tmp[:, tsl * UNIT:tsl * UNIT + 512], n == 0, n == NST - 1)
                    mm(psY[:, 512:UNIT], s_id[:],
                       s_tmp[:, tsl * UNIT + 512:(tsl + 1) * UNIT], n == 0,
                       n == NST - 1)
                    rg_tmp.mark(cP)
            last_tts[0] = cV.n
            rg_yev.acquire("scalar")
            P.wait("scalar", cP)
            for cti in range(2):
                psY = psY0 if cti == 0 else psY1
                act(s_yev[:, cti * UNIT:(cti + 1) * UNIT], psY[:, :], AF.Copy)
            yev_prev[0] = cA.n
            if DBG and k == 0 and u == 0:
                P.wait("gpsimd", cA)
                dmaG(dbg["yev"], s_yev[:, :])
                dmaG(dbg["dl"], s_dl[:, :])
            P.wait("gpsimd", cA)
            for cti in range(2):
                for hf in range(2):
                    dmaG(yAi[k].ap()[2 * u + hf, cti],
                         s_yev[:, cti * UNIT + hf * 512:cti * UNIT + (hf + 1) * 512])
            rg_yev.mark(qG)
        P.wait("gpsimd", qG)
        coll("AllToAll", yAi[k].ap().opt(), yAo[k].ap().opt())
        cc_y[k] = cc.n

    def back_end(k):
        first = (k == 0)
        P.wait("sync", cc, cc_y[k])
        P.wait("sync", cV)
        P.wait("gpsimd", cP)
        ytiles = []
        for ct in range(CT):
            if ct < 8:
                dst = s_rr[:, ct * SLAB:(ct + 1) * SLAB]
            else:
                dst = s_ka[:, (ct - 8) * SLAB:(ct - 7) * SLAB]
            ytiles.append(dst)
            rg = rg_be.acquire("sync")
            dmaS(s_ybe[:, rg * SLAB:(rg + 1) * SLAB], yAo[k].ap()[ct // 2, ct % 2])
            dmaS(s_xcr[:, rg * SLAB:(rg + 1) * SLAB], xc_st.ap()[k, ct])
            dmaS(s_szr[:, rg * SLAB:(rg + 1) * SLAB], zp_st.ap()[k, ct])
            yld = qS.n
            if DBG and k == 0 and ct < 2:
                P.wait("gpsimd", qS, yld)
                dmaG(dbg["ybe"][:, ct * SLAB:(ct + 1) * SLAB],
                     s_ybe[:, rg * SLAB:(rg + 1) * SLAB])
            P.wait("scalar", qS, yld)
            act(s_szr[:, rg * SLAB:(rg + 1) * SLAB],
                s_szr[:, rg * SLAB:(rg + 1) * SLAB], AF.Silu)
            sz_n = cA.n
            P.wait("gpsimd", qS, yld)
            P.wait("gpsimd", cA, sz_n)
            tmpv = s_tmp[:, 0:SLAB] if rg == 0 else s_tmp[:, UNIT:UNIT + SLAB]
            P.op("gpsimd", lambda e, tmpv=tmpv, rg=rg, ct=ct: e.tensor_scalar(
                out=tmpv, in0=s_xcr[:, rg * SLAB:(rg + 1) * SLAB],
                scalar1=s_kc[:, O_DV + ct:O_DV + ct + 1], scalar2=None,
                op0=OP.mult), cG)
            gtt(tmpv, tmpv, s_ybe[:, rg * SLAB:(rg + 1) * SLAB], op=OP.add)
            gtt(dst, tmpv, s_szr[:, rg * SLAB:(rg + 1) * SLAB])
            rg_be.mark(cG)
        yt_all = cG.n
        if DBG and k == 0:
            P.wait("gpsimd", cG, yt_all)
            dmaG(dbg["yt"][0], s_rr[:, :])
            dmaG(dbg["yt"][1], s_ka[:, :])
        P.wait("tensor", cG, yt_all)
        xm_prev = [0]
        for g in range(3):
            mts = list(range(g * 3, min(DCH, (g + 1) * 3)))
            w = len(mts) * 128
            for ct in range(CT):
                sl = rg_wpB.acquire("sync")
                dmaS(s_wpB[:, sl * 512:sl * 512 + w],
                     woutS.ap()[ct][:, mts[0] * 128:mts[0] * 128 + w])
                P.wait("tensor", qS)
                P.wait("tensor", cV, xm_prev[0])
                for mi, mt_ in enumerate(mts):
                    wp = s_wpB[:, sl * 512 + mi * 128:sl * 512 + (mi + 1) * 128]
                    ldw(wp)
                    mm((psA, psB, psC)[mi][:, :], wp, ytiles[ct],
                       ct == 0, ct == CT - 1)
                rg_wpB.mark(cP)
            for mi, mt_ in enumerate(mts):
                x2s = rg_x2.acquire("sync")
                dmaS(s_x2[:, x2s * SLAB:(x2s + 1) * SLAB],
                     xT[mt_ * 128:(mt_ + 1) * 128, k * SLAB:(k + 1) * SLAB])
                P.wait("vector", qS)
                P.wait("vector", cP)
                vstt(s_xm[:, mt_ * SLAB:(mt_ + 1) * SLAB],
                     (psA, psB, psC)[mi][:, :],
                     s_mod[:, M_GA + mt_:M_GA + mt_ + 1],
                     s_x2[:, x2s * SLAB:(x2s + 1) * SLAB],
                     OP.mult, OP.add)
                rg_x2.mark(cV)
            xm_prev[0] = cV.n
        xm_n = cV.n
        if DBG and k == 0:
            P.wait("gpsimd", cV, xm_n)
            dmaG(dbg["xm"], s_xm[:, :])
        P.wait("gpsimd", cV, xm_n)
        for j in range(DCH):
            xmv = s_xm[:, j * SLAB:(j + 1) * SLAB]
            xbv = s_xb[:, j * WIN:j * WIN + SLAB]
            gcp(xbv, xmv)
            sq = rg_sq.acquire("gpsimd")
            sqv = s_sq[:, sq * WIN:sq * WIN + SLAB]
            gtt(sqv, xmv, xmv)
            P.wait("tensor", cG)
            ldw(s_one[:])
            mm(psD[0:1, 0:512], s_one[:], xbv, j == 0, j == DCH - 1)
            mm(psC[0:1, 0:512], s_one[:], sqv, j == 0, j == DCH - 1)
            rg_sq.mark(cP)
        P.wait("vector", cP)
        vts(s_row[0:1, R_M:R_M + 512], psD[0:1, 0:512], 1.0 / D)
        vts(s_row[0:1, R_S:R_S + 512], psC[0:1, 0:512], 1.0 / D)
        vtt(s_row[0:1, R_V:R_V + 512], s_row[0:1, R_M:R_M + 512],
            s_row[0:1, R_M:R_M + 512])
        vtt(s_row[0:1, R_V:R_V + 512], s_row[0:1, R_S:R_S + 512],
            s_row[0:1, R_V:R_V + 512], op=OP.subtract)
        P.wait("scalar", cV)
        act(s_row[0:1, R_I:R_I + 512], s_row[0:1, R_V:R_V + 512], AF.Ln, bias=EPS)
        act(s_row[0:1, R_I:R_I + 512], s_row[0:1, R_I:R_I + 512], AF.Exp, scale=-0.5)
        P.wait("gpsimd", cA)
        dmaG(rowS.ap()[2][0:512].unsqueeze(0), s_row[0:1, R_M:R_M + 512])
        dmaG(rowS.ap()[3][0:512].unsqueeze(0), s_row[0:1, R_I:R_I + 512])
        P.wait("sync", qG)
        dmaS(s_mrep[:, 0:512], rowS.ap()[2][0:512].unsqueeze(0).broadcast_to((128, 512)))
        dmaS(s_irep[:, 0:512], rowS.ap()[3][0:512].unsqueeze(0).broadcast_to((128, 512)))
        P.wait("vector", qS)
        for j in range(DCH):
            vtt(s_t1[:, 0:SLAB], s_xm[:, j * SLAB:(j + 1) * SLAB], s_mrep[:, 0:512],
                op=OP.subtract)
            vstt(s_l2[:, j * (SLAB + 1) + 1:(j + 1) * (SLAB + 1)], s_t1[:, 0:SLAB],
                 s_mod[:, M_CB1 + j:M_CB1 + j + 1], s_irep[:, 0:512],
                 OP.mult, OP.mult)
        l2_n = cV.n
        for j in range(DCH):
            vcp(s_lc[:, j:j + 1],
                s_l2[:, (j + 1) * (SLAB + 1) - 1:(j + 1) * (SLAB + 1)])
        P.wait("gpsimd", cV)
        dmaG(agli[k].ap().rearrange("(p j) -> p j", p=128), s_lc[:, 0:DCH])
        P.wait("gpsimd", qG)
        coll("AllGather", agli[k].ap().opt(), aglo[k].ap().opt())
        cc_agl[k] = cc.n
        P.wait("gpsimd", cc, cc_agl[k])

        def prevcol(e, k=k):
            rank = e.cc_rank([list(range(NCORE))])
            for r in e.Switch(rank, NCORE):
                if r == 0:
                    if k == 0:
                        e.memset(s_lc[:, 0:DCH], 0.0).then_inc(qG.sem, 16)
                    else:
                        e.dma_start(out=s_lc[:, 0:DCH],
                                    in_=aglo[k - 1].ap()[NCORE - 1].rearrange(
                                        "(p j) -> p j", p=128)).then_inc(qG.sem, 16)
                else:
                    e.dma_start(out=s_lc[:, 0:DCH],
                                in_=aglo[k].ap()[r - 1].rearrange(
                                    "(p j) -> p j", p=128)).then_inc(qG.sem, 16)
        P.raw("gpsimd", prevcol)
        qG.n += 16
        P.wait("vector", qG)
        for j in range(DCH):
            vcp(s_l2[:, j * (SLAB + 1):j * (SLAB + 1) + 1], s_lc[:, j:j + 1])
        l2c_n = cV.n

        P.wait("vector", cV, l2c_n)
        for j in range(DCH):
            l2v = s_l2[:, j * (SLAB + 1):(j + 1) * (SLAB + 1)]
            vts(s_t1[:, 0:SLAB], l2v[:, 1:SLAB + 1], s_kc[:, O_MUR + j:O_MUR + j + 1])
            P.op("vector", lambda e, j=j, l2v=l2v: e.scalar_tensor_tensor(
                out=s_xb[:, j * WIN:j * WIN + SLAB], in0=l2v[:, 0:SLAB],
                scalar=s_mu1[:, 0 + j:0 + j + 1], in1=s_t1[:, 0:SLAB],
                op0=OP.mult, op1=OP.add), cV)
            vts(s_t1[:, SLAB:2 * SLAB], l2v[:, 1:SLAB + 1],
                s_kc[:, O_MUK + j:O_MUK + j + 1])
            P.op("vector", lambda e, j=j, l2v=l2v: e.scalar_tensor_tensor(
                out=s_h[:, j * WIN:j * WIN + SLAB], in0=l2v[:, 0:SLAB],
                scalar=s_mu1[:, 8 + j:8 + j + 1], in1=s_t1[:, SLAB:2 * SLAB],
                op0=OP.mult, op1=OP.add), cV)
        xrk_n = cV.n
        if DBG and k == 0:
            P.wait("gpsimd", cV, xrk_n)
            dmaG(dbg["l2"], s_l2[:, :])
        P.wait("tensor", cV, xrk_n)
        rk_prev = [0]
        for which, (wscr, bias, corr, xsrc, dst) in enumerate((
                (wrS, s_br, s_cr, s_xb, s_rr),
                (wkS, s_bk, s_ck, s_h, s_ka))):
            for g in range(3):
                mts = list(range(g * 3, min(DCH, (g + 1) * 3)))
                w = len(mts) * 128
                for j in range(DCH):
                    sl = rg_wpB.acquire("sync")
                    dmaS(s_wpB[:, sl * 512:sl * 512 + w],
                         wscr.ap()[j][:, mts[0] * 128:mts[0] * 128 + w])
                    P.wait("tensor", qS)
                    P.wait("tensor", cA, rk_prev[0])
                    for mi, mt_ in enumerate(mts):
                        wp = s_wpB[:, sl * 512 + mi * 128:sl * 512 + (mi + 1) * 128]
                        ldw(wp)
                        mm((psA, psB, psC)[mi][:, :], wp,
                           xsrc[:, j * WIN:j * WIN + SLAB], j == 0, j == DCH - 1)
                    rg_wpB.mark(cP)
                P.wait("vector", cP)
                for mi, mt_ in enumerate(mts):
                    pp = (psA, psB, psC)[mi]
                    P.op("vector", lambda e, pp=pp, bias=bias, mt_=mt_:
                         e.tensor_scalar(out=pp[:, :], in0=pp[:, :],
                                         scalar1=bias[:, mt_:mt_ + 1],
                                         scalar2=None, op0=OP.add), cV)
                    if first:
                        P.op("vector", lambda e, pp=pp, corr=corr, mt_=mt_:
                             e.tensor_tensor(out=pp[:, 0:1], in0=pp[:, 0:1],
                                             in1=corr[:, mt_:mt_ + 1], op=OP.add), cV)
                P.wait("scalar", cV)
                for mi, mt_ in enumerate(mts):
                    pp = (psA, psB, psC)[mi]
                    if which == 0:
                        act(dst[:, mt_ * SLAB:(mt_ + 1) * SLAB], pp[:, :], AF.Sigmoid)
                    else:
                        act(s_da[:, 0:512], pp[:, :], AF.Relu)
                        act(dst[:, mt_ * SLAB:(mt_ + 1) * SLAB], s_da[:, 0:512],
                            AF.Square)
                rk_prev[0] = cA.n
        rk_n = cA.n
        if DBG and k == 0:
            P.wait("gpsimd", cA, rk_n)
            dmaG(dbg["rr"], s_rr[:, :])
            dmaG(dbg["ka"], s_ka[:, :])
        P.wait("tensor", cA, rk_n)
        fin_prev = [0]
        for g in range(3):
            mts = list(range(g * 3, min(DCH, (g + 1) * 3)))
            w = len(mts) * 128
            for j in range(DCH):
                sl = rg_wpB.acquire("sync")
                dmaS(s_wpB[:, sl * 512:sl * 512 + w],
                     wvS.ap()[j][:, mts[0] * 128:mts[0] * 128 + w])
                P.wait("tensor", qS)
                P.wait("tensor", cV, fin_prev[0])
                for mi, mt_ in enumerate(mts):
                    wp = s_wpB[:, sl * 512 + mi * 128:sl * 512 + (mi + 1) * 128]
                    ldw(wp)
                    mm((psA, psB, psC)[mi][:, :], wp, s_ka[:, j * SLAB:(j + 1) * SLAB],
                       j == 0, j == DCH - 1)
                rg_wpB.mark(cP)
            P.wait("vector", cP)
            for mi, mt_ in enumerate(mts):
                pp = (psA, psB, psC)[mi]
                t1s = rg_t1.acquire("vector")
                vtt(s_t1[:, t1s * SLAB:(t1s + 1) * SLAB],
                    s_rr[:, mt_ * SLAB:(mt_ + 1) * SLAB], pp[:, :])
                P.op("vector", lambda e, mt_=mt_, t1s=t1s: e.scalar_tensor_tensor(
                    out=s_t1[:, t1s * SLAB:(t1s + 1) * SLAB],
                    in0=s_t1[:, t1s * SLAB:(t1s + 1) * SLAB],
                    scalar=s_mod[:, M_GB + mt_:M_GB + mt_ + 1],
                    in1=s_xm[:, mt_ * SLAB:(mt_ + 1) * SLAB],
                    op0=OP.mult, op1=OP.add), cV)
                P.wait("sync", cV)
                dmaS(outT[mt_ * 128:(mt_ + 1) * 128, k * SLAB:(k + 1) * SLAB],
                     s_t1[:, t1s * SLAB:(t1s + 1) * SLAB])
                rg_t1.mark(qS)
                fin_prev[0] = cV.n

    stage = 1
    for k in range(ROUNDS):
        if stage >= nstage:
            break
        front_end(k, fesub)
        stage += 1
        if stage >= nstage:
            break
        scan(k)
        stage += 1
        if stage >= nstage:
            break
        back_end(k)
        stage += 1

    P.wait("sync", qS)
    P.wait("sync", qG)
    P.wait("sync", cc)
    P.wait("gpsimd", qS)
    P.wait("gpsimd", qG)

    with nc.Block() as block:
        @block.sync
        def _(eng):
            P.emit("sync", eng)

        @block.tensor
        def _(eng):
            P.emit("tensor", eng)

        @block.vector
        def _(eng):
            P.emit("vector", eng)

        @block.scalar
        def _(eng):
            P.emit("scalar", eng)

        @block.gpsimd
        def _(eng):
            P.emit("gpsimd", eng)

    for t in reversed(ctx):
        t.__exit__(None, None, None)
    return nc


_cache = {}


def make_in_maps(L0_attn_x, L0_attn_W_in, L0_attn_conv_w, L0_attn_conv_b,
                 L0_attn_W_x, L0_attn_W_dt, L0_attn_b_dt, L0_attn_A_log,
                 L0_attn_D, L0_attn_W_out, L0_ffn_W_r, L0_ffn_W_k, L0_ffn_W_v,
                 L0_ffn_mu_r, L0_ffn_mu_k, L0_c, L0_wmod):
    x = np.ascontiguousarray(np.asarray(L0_attn_x, np.float32).reshape(TOK, D).T)

    cw = np.asarray(L0_attn_conv_w, np.float32).reshape(CI, 4)
    convd = np.zeros((4 * CT, 128, 128), np.float32)
    for kk in range(4):
        for ct in range(CT):
            d = cw[ct * 128:(ct + 1) * 128, kk]
            convd[kk * CT + ct][np.arange(128), np.arange(128)] = d

    ident = np.eye(128, dtype=np.float32)
    ones = np.ones((128, 1), np.float32)

    def cols(v, n):
        return np.asarray(v, np.float32).reshape(n, 128).T

    in_maps = []
    for r in range(NCORE):
        xr = np.zeros((D, XHALO), np.float32)
        for k in range(ROUNDS):
            t0 = k * RTOK + r * SLAB
            xr[:, k * SLAB:(k + 1) * SLAB] = x[:, t0:t0 + SLAB]
            if t0 > 0:
                xr[:, XCOLS + k * 3:XCOLS + (k + 1) * 3] = x[:, t0 - 3:t0]

        mask0 = np.full((128, 1), 1.0 if r == 0 else 0.0, np.float32)
        maskh = np.full((128, 1), 0.0 if r == 0 else 1.0, np.float32)
        KCm = np.concatenate([
            cols(np.asarray(L0_c, np.float32).reshape(D), DCH),
            cols(L0_ffn_mu_r, DCH), cols(L0_ffn_mu_k, DCH),
            cols(L0_attn_conv_b, CT), cols(L0_attn_D, CT),
            cols(np.asarray(L0_attn_b_dt, np.float32)[r * 256:(r + 1) * 256], 2),
            np.asarray(L0_attn_A_log, np.float32)[r * 256:(r + 1) * 256, :].reshape(
                2, 128, NST).transpose(1, 0, 2).reshape(128, 32),
            ones, mask0, maskh, ident,
            np.full((128, 1), EPS, np.float32),
            np.full((128, 1), 1.0, np.float32),
            np.full((128, 1), -0.5, np.float32)], axis=1).astype(np.float32)

        in_maps.append({
            "xT": np.ascontiguousarray(xr),
            "Win": np.asarray(L0_attn_W_in, np.float32),
            "convd": convd,
            "Wx": np.asarray(L0_attn_W_x, np.float32),
            "WdtS": np.ascontiguousarray(
                np.asarray(L0_attn_W_dt, np.float32)[:, r * 256:(r + 1) * 256]),
            "Wout": np.asarray(L0_attn_W_out, np.float32),
            "WrT": np.ascontiguousarray(np.asarray(L0_ffn_W_r, np.float32).T),
            "WkT": np.ascontiguousarray(np.asarray(L0_ffn_W_k, np.float32).T),
            "WvT": np.ascontiguousarray(np.asarray(L0_ffn_W_v, np.float32).T),
            "wmodS": np.ascontiguousarray(
                np.asarray(L0_wmod, np.float32)[:, r * 768:(r + 1) * 768]),
            "KC": KCm,
        })
    return in_maps


def kernel(**inputs):
    in_maps = make_in_maps(**inputs)
    if "prog" not in _cache:
        _cache["prog"] = build_program()
    res = run_bass_kernel_spmd(_cache["prog"], in_maps, core_ids=list(range(NCORE)))

    out = np.zeros((TOK, D), np.float32)
    for r in range(NCORE):
        oT = res.results[r]["outT"]
        for k in range(ROUNDS):
            t0 = k * RTOK + r * SLAB
            out[t0:t0 + SLAB, :] = oT[:, k * SLAB:(k + 1) * SLAB].T
    return out.reshape(1, TOK, D)




# version 08
# speedup vs baseline: 1.7680x; 1.7680x over previous
import numpy as np
import concourse.bass as bass
import concourse.mybir as mybir
from concourse.alu_op_type import AluOpType
from concourse.bass_utils import run_bass_kernel_spmd
from concourse.aot_env import allow_ici_all_to_all

F32 = mybir.dt.float32
BF16 = mybir.dt.bfloat16
AF = mybir.ActivationFunctionType
OP = AluOpType

DBG = False
PARANOID = False
PARMODE = "phase"
import os as _os
GUARD_FE = _os.environ.get("GUARD_FE", "1") == "1"
GUARD_SC = _os.environ.get("GUARD_SC", "0") == "1"
GUARD_BE = _os.environ.get("GUARD_BE", "1") == "1"
NCORE = 8
TOK = 16384
D = 1024
CI = 2048
NST = 16
DTR = 64
RTOK = 4096
ROUNDS = TOK // RTOK
SLAB = 512
UNIT = 1024
UPR = RTOK // UNIT
DCH = D // 128
MT = 4096 // 128
CT = CI // 128
EPS = 1e-5
WIN = SLAB + 3
XCOLS = ROUNDS * SLAB
XHALO = XCOLS + ROUNDS * 3


class Ctr:
    def __init__(self, sem):
        self.sem = sem
        self.n = 0


class Plan:
    ENGS = ("sync", "tensor", "vector", "scalar", "gpsimd")

    def __init__(self):
        self.streams = {e: [] for e in self.ENGS}
        self.seen = {e: {} for e in self.ENGS}
        self.own = {}
        self.dma_guard = False

    def wait(self, eng, ctr, val=None):
        v = ctr.n if val is None else val
        if v <= 0:
            return
        if self.seen[eng].get(id(ctr), 0) >= v:
            return
        self.seen[eng][id(ctr)] = v
        self.streams[eng].append(("w", ctr, v))

    def op(self, eng, fn, inc, inck=1):
        own = self.own.get(eng)
        if own is not None and own.n > 0:
            self.wait(eng, own, own.n)
        if PARANOID:
            for c in self.ctrs.values():
                self.wait(eng, c, c.n)
        elif PARMODE == "dma" or self.dma_guard:
            if inc is not None and inck == 16:
                for nm in ("cP", "cA", "cV", "cG"):
                    self.wait(eng, self.ctrs[nm], self.ctrs[nm].n)
            else:
                for nm in ("qS", "qG"):
                    self.wait(eng, self.ctrs[nm], self.ctrs[nm].n)

        self.streams[eng].append(("o", fn, inc, inck))
        if inc is not None:
            inc.n += inck
            return inc.n
        return None

    def phase_barrier(self):
        for eng in self.ENGS:
            for c in self.ctrs.values():
                self.wait(eng, c, c.n)

    def dma(self, eng, fn, inc):
        return self.op(eng, fn, inc, 16)

    def raw(self, eng, fn):
        self.streams[eng].append(("r", fn))

    def emit(self, eng_name, eng):
        for item in self.streams[eng_name]:
            if item[0] == "w":
                eng.wait_ge(item[1].sem, item[2])
            elif item[0] == "r":
                item[1](eng)
            else:
                _, fn, inc, inck = item
                ins = fn(eng)
                if inc is not None:
                    ins.then_inc(inc.sem, inck)


class Ring:
    def __init__(self, plan, nslots):
        self.P = plan
        self.n = nslots
        self.marks = [[] for _ in range(nslots)]
        self.i = 0

    def acquire(self, eng):
        s = self.i % self.n
        for (ctr, v) in self.marks[s]:
            self.P.wait(eng, ctr, v)
        self.marks[s] = []
        self.i += 1
        return s

    def mark(self, ctr, val=None):
        self.marks[(self.i - 1) % self.n].append(
            (ctr, ctr.n if val is None else val))


def build_program(nstage=99, fesub=99):
    nc = bass.Bass("TRN2", target_bir_lowering=False, debug=False)
    P = Plan()

    def din(name, shape):
        return nc.dram_tensor(name, shape, F32, kind="ExternalInput").ap()

    xT = din("xT", [D, XHALO])
    Win = din("Win", [D, 4096])
    convd = din("convd", [4 * CT, 128, 128])
    Wx = din("Wx", [CI, 96])
    WdtS = din("WdtS", [DTR, 256])
    Wout = din("Wout", [CI, D])
    WrT = din("WrT", [D, D])
    WkT = din("WkT", [D, D])
    WvT = din("WvT", [D, D])
    wmodS = din("wmodS", [D, 768])
    KC = din("KC", [128, 224])
    O_CV, O_MUR, O_MUK, O_CB, O_DV, O_BDT, O_AL, O_ONE, O_MSK, O_MSKH, O_ID = \
        0, 8, 16, 24, 40, 56, 58, 90, 91, 92, 93
    O_EPS, O_ONEF, O_NHALF = 221, 222, 223

    outT = nc.dram_tensor("outT", [D, XCOLS], F32, kind="ExternalOutput").ap()
    dbg = {}
    if DBG:
        dbg["h"] = nc.dram_tensor("dbg_h", [128, DCH * WIN], BF16, kind="ExternalOutput").ap()
        dbg["xm"] = nc.dram_tensor("dbg_xm", [128, DCH * SLAB], F32, kind="ExternalOutput").ap()
        dbg["l2"] = nc.dram_tensor("dbg_l2", [128, DCH * (SLAB + 1)], BF16, kind="ExternalOutput").ap()
        dbg["rr"] = nc.dram_tensor("dbg_rr", [128, DCH * SLAB], BF16, kind="ExternalOutput").ap()
        dbg["ka"] = nc.dram_tensor("dbg_ka", [128, DCH * SLAB], BF16, kind="ExternalOutput").ap()
        dbg["dl"] = nc.dram_tensor("dbg_dl", [128, 2 * UNIT], F32, kind="ExternalOutput").ap()
        dbg["yev"] = nc.dram_tensor("dbg_yev", [128, 2 * UNIT], BF16, kind="ExternalOutput").ap()
        dbg["yt"] = nc.dram_tensor("dbg_yt", [2, 128, DCH * SLAB], BF16, kind="ExternalOutput").ap()
        dbg["mod"] = nc.dram_tensor("dbg_mod", [128, 48], F32, kind="ExternalOutput").ap()
        dbg["ybe"] = nc.dram_tensor("dbg_ybe", [128, 2 * SLAB], BF16, kind="ExternalOutput").ap()
        dbg["row"] = nc.dram_tensor("dbg_row", [1, 2080], F32, kind="ExternalOutput").ap()
        dbg["agmo"] = nc.dram_tensor("dbg_agmo", [6144], F32, kind="ExternalOutput").ap()
        dbg["bc"] = nc.dram_tensor("dbg_bc", [128, 40], BF16, kind="ExternalOutput").ap()
        dbg["spr"] = nc.dram_tensor("dbg_spr", [128, 24], F32, kind="ExternalOutput").ap()
        dbg["xc"] = nc.dram_tensor("dbg_xc", [128, CT * WIN], BF16, kind="ExternalOutput").ap()

    winS = nc.dram_tensor("winS", [DCH, 128, 4096], BF16)
    woutS = nc.dram_tensor("woutS", [CT, 128, D], BF16)
    wrS = nc.dram_tensor("wrS", [DCH, 128, D], BF16)
    wkS = nc.dram_tensor("wkS", [DCH, 128, D], BF16)
    wvS = nc.dram_tensor("wvS", [DCH, 128, D], BF16)
    vecS = nc.dram_tensor("vecS", [8, 4096], F32)
    rowS = nc.dram_tensor("rowS", [4, WIN], F32)
    xc_st = nc.dram_tensor("xc_st", [ROUNDS, CT, 128, SLAB], BF16)
    zp_st = nc.dram_tensor("zp_st", [ROUNDS, CT, 128, SLAB], BF16)
    xcAi = [nc.dram_tensor(f"xcAi{k}", [NCORE, 2, 128, SLAB], BF16) for k in range(ROUNDS)]
    xcAo = [nc.dram_tensor(f"xcAo{k}", [NCORE, 2, 128, SLAB], BF16) for k in range(ROUNDS)]
    agxi = [nc.dram_tensor(f"agxi{k}", [96, SLAB], BF16) for k in range(ROUNDS)]
    agxo = [nc.dram_tensor(f"agxo{k}", [NCORE, 96, SLAB], BF16) for k in range(ROUNDS)]
    yAi = [nc.dram_tensor(f"yAi{k}", [NCORE, 2, 128, SLAB], BF16) for k in range(ROUNDS)]
    yAo = [nc.dram_tensor(f"yAo{k}", [NCORE, 2, 128, SLAB], BF16) for k in range(ROUNDS)]
    agli = [nc.dram_tensor(f"agli{k}", [D], BF16) for k in range(ROUNDS)]
    aglo = [nc.dram_tensor(f"aglo{k}", [NCORE, D], BF16) for k in range(ROUNDS)]
    agmi = nc.dram_tensor("agmi", [768], F32)
    agmo = nc.dram_tensor("agmo", [6144], F32)

    ctx = []

    def sb(name, pr, cols, dt):
        t = nc.sbuf_tensor(name, [pr, cols], dt)
        ctx.append(t)
        return t.__enter__()

    def psm(name, cols):
        t = nc.psum_tensor(name, [128, cols], F32)
        ctx.append(t)
        return t.__enter__()

    def sem(name):
        t = nc.semaphore(name)
        ctx.append(t)
        return Ctr(t.__enter__())

    s_kc = sb("s_kc", 128, 224, F32)
    s_asc = sb("s_asc", 128, 32, F32)
    s_mod = sb("s_mod", 128, 48, F32)
    M_SA, M_CA1, M_GA, M_SB, M_CB1, M_GB = 0, 8, 16, 24, 32, 40
    s_mu1 = sb("s_mu1", 128, 16, F32)
    s_bxz = sb("s_bxz", 128, MT, F32)
    s_br = sb("s_br", 128, DCH, F32)
    s_bk = sb("s_bk", 128, DCH, F32)
    s_cr = sb("s_cr", 128, DCH, F32)
    s_ck = sb("s_ck", 128, DCH, F32)
    s_bc = sb("s_bc", 128, 40, BF16)
    B_SIL, B_SA, B_SB, B_MBR, B_MBK = 0, 8, 16, 24, 32

    s_diag = sb("s_diag", 128, 4 * CT * 128, BF16)
    s_wx = sb("s_wx", 128, CT * 96, BF16)
    s_wdt = sb("s_wdt", DTR, 256, BF16)
    s_id = sb("s_id", 128, 128, BF16)
    s_one = sb("s_one", 128, 1, BF16)
    s_wpF = sb("s_wpF", 128, 2 * 512, BF16)
    s_wpB = sb("s_wpB", 128, 2 * 512, BF16)

    s_xw = sb("s_xw", 128, 2 * WIN, F32)
    s_xb = sb("s_xb", 128, DCH * WIN, BF16)
    s_sq = sb("s_sq", 128, 2 * WIN, BF16)
    s_row = sb("s_row", 1, 2080, F32)
    R_M, R_S, R_V, R_I = 0, 520, 1040, 1560
    s_mrep = sb("s_mrep", 128, WIN, F32)
    s_irep = sb("s_irep", 128, WIN, F32)
    s_h = sb("s_h", 128, DCH * WIN, BF16)
    s_xe = sb("s_xe", 128, CT * WIN, BF16)
    s_zx = sb("s_zx", 128, 4 * SLAB, BF16)
    s_xdb = sb("s_xdb", 96, SLAB, BF16)

    s_dr = sb("s_dr", DTR, 2 * UNIT, BF16)
    s_dl = sb("s_dl", 128, 2 * UNIT, F32)
    s_dlb = sb("s_dlb", 128, 2 * UNIT, F32)
    s_xcu = sb("s_xcu", 128, 2 * UNIT, BF16)
    s_u = sb("s_u", 128, 2 * UNIT, BF16)
    s_brp = sb("s_brp", 128, 2 * UNIT, BF16)
    s_crp = sb("s_crp", 128, 2 * UNIT, BF16)
    s_da = sb("s_da", 128, 2 * UNIT, F32)
    s_d2 = sb("s_d2", 128, 2 * UNIT, BF16)
    s_hs = sb("s_hs", 128, 4 * UNIT, BF16)
    s_tmp = sb("s_tmp", 128, 4 * UNIT, BF16)
    s_car = sb("s_car", 128, 2 * NST, F32)
    s_yev = sb("s_yev", 128, 2 * UNIT, BF16)

    s_ybe = sb("s_ybe", 128, 2 * SLAB, BF16)
    s_xcr = sb("s_xcr", 128, 2 * SLAB, BF16)
    s_szr = sb("s_szr", 128, 2 * SLAB, BF16)
    s_x2 = sb("s_x2", 128, 2 * SLAB, F32)
    s_xm = sb("s_xm", 128, DCH * SLAB, F32)
    s_l2 = sb("s_l2", 128, DCH * (SLAB + 1), BF16)
    s_rr = sb("s_rr", 128, DCH * SLAB, BF16)
    s_ka = sb("s_ka", 128, DCH * SLAB, BF16)
    s_t1 = sb("s_t1", 128, 2 * SLAB, F32)
    s_lc = sb("s_lc", 128, DCH, BF16)
    s_silc = sb("s_silc", 128, DCH, F32)
    s_tr = sb("s_tr", 48, 128, F32)
    s_pr = sb("s_pr", 128, 1024, F32)

    psA = psm("psA", 512)
    psB = psm("psB", 512)
    psC = psm("psC", 512)
    psD = psm("psD", 512)
    psY0 = psm("psY0", UNIT)
    psY1 = psm("psY1", UNIT)

    nc.const_aps.register(F32, EPS, s_kc[:, 221:222])
    nc.const_aps.register(F32, 1.0, s_kc[:, 222:223])
    nc.const_aps.register(F32, -0.5, s_kc[:, 223:224])

    qS = sem("qS")
    qG = sem("qG")
    cP = sem("cP")
    cA = sem("cA")
    cV = sem("cV")
    cG = sem("cG")
    cc = sem("cc")
    P.ctrs = dict(qS=qS, qG=qG, cP=cP, cA=cA, cV=cV, cG=cG, cc=cc)
    P.own = {"vector": cV, "scalar": cA, "gpsimd": cG}

    def dmaS(dst, src):
        return P.dma("sync", lambda e, d=dst, s=src: e.dma_start(out=d, in_=s), qS)

    def dmaG(dst, src):
        return P.dma("gpsimd", lambda e, d=dst, s=src: e.dma_start(out=d, in_=s), qG)

    def mm(out, lhsT, rhs, start, stop):
        return P.op("tensor", lambda e, o=out, l=lhsT, r=rhs, s1=start, s2=stop:
                    e.matmul(o, l, r, start=s1, stop=s2), cP)

    def act(out, in_, func, bias=None, scale=None):
        kw = {}
        if bias is not None:
            kw["bias"] = bias
        if scale is not None:
            kw["scale"] = scale
        return P.op("scalar", lambda e, o=out, i=in_, f=func, kw=kw:
                    e.activation(out=o, in_=i, func=f, **kw), cA)

    def vtt(out, a, b, op=OP.mult):
        return P.op("vector", lambda e, o=out, x=a, y=b, p=op:
                    e.tensor_tensor(out=o, in0=x, in1=y, op=p), cV)

    def gtt(out, a, b, op=OP.mult):
        return P.op("gpsimd", lambda e, o=out, x=a, y=b, p=op:
                    e.tensor_tensor(out=o, in0=x, in1=y, op=p), cG)

    def vts(out, in_, s1, s2=None, op0=OP.mult, op1=None):
        def f(e, o=out, i=in_, s1=s1, s2=s2, op0=op0, op1=op1):
            if op1 is None:
                return e.tensor_scalar(out=o, in0=i, scalar1=s1, scalar2=None, op0=op0)
            return e.tensor_scalar(out=o, in0=i, scalar1=s1, scalar2=s2, op0=op0, op1=op1)
        return P.op("vector", f, cV)

    def vstt(out, in0, scalar, in1, op0, op1):
        return P.op("vector", lambda e, o=out, a=in0, s=scalar, b=in1, p0=op0, p1=op1:
                    e.scalar_tensor_tensor(out=o, in0=a, scalar=s, in1=b, op0=p0, op1=p1), cV)

    def gstt(out, in0, scalar, in1, op0, op1):
        return P.op("gpsimd", lambda e, o=out, a=in0, s=scalar, b=in1, p0=op0, p1=op1:
                    e.scalar_tensor_tensor(out=o, in0=a, scalar=s, in1=b, op0=p0, op1=p1), cG)

    def vcp(out, in_):
        return P.op("vector", lambda e, o=out, i=in_: e.tensor_copy(o, i), cV)

    def gcp(out, in_):
        return P.op("gpsimd", lambda e, o=out, i=in_: e.tensor_copy(o, i), cG)

    def acp(out, in_):
        return P.op("scalar", lambda e, o=out, i=in_: e.copy(out=o, in_=i), cA)

    def ldw(w):
        P.op("tensor", lambda e, x=w: e.ldweights(x), None)

    def coll(kind, ins, outs):
        def f(e, k=kind, i=ins, o=outs):
            with allow_ici_all_to_all():
                return e.collective_compute(
                    k, OP.bypass, replica_groups=[list(range(NCORE))],
                    ins=[i], outs=[o])
        return P.op("gpsimd", f, cc)

    def cols_from_hbm(src_flat, ncols, dst_ap):
        P.wait("sync", cP)
        dmaS(s_tr[0:ncols, 0:128], src_flat.rearrange("(a b) -> a b", b=128))
        P.wait("tensor", qS)
        P.op("tensor", lambda e, n=ncols: e.transpose(
            psD[:, 0:n], s_tr[0:n, 0:128], s_kc[0:n, O_ID:O_ID + n]), cP)
        P.wait("scalar", cP)
        acp(dst_ap, psD[:, 0:ncols])

    dmaS(s_kc[:, :], KC)
    P.wait("vector", qS)
    P.wait("scalar", qS)
    vcp(s_one[:], s_kc[:, O_ONE:O_ONE + 1])
    vcp(s_id[:], s_kc[:, O_ID:O_ID + 128])
    act(s_asc[:], s_kc[:, O_AL:O_AL + 32], AF.Exp)
    P.wait("vector", cA)
    vts(s_asc[:], s_asc[:], -1.0)
    vts(s_mu1[:, 0:DCH], s_kc[:, O_MUR:O_MUR + DCH], -1.0, 1.0, OP.mult, OP.add)
    vts(s_mu1[:, 8:8 + DCH], s_kc[:, O_MUK:O_MUK + DCH], -1.0, 1.0, OP.mult, OP.add)

    def conv_copy(src_ap, dst_ap, tmpcols):
        P.wait("sync", cV)
        dmaS(s_pr[:, 0:tmpcols], src_ap)
        P.wait("vector", qS)
        vcp(dst_ap, s_pr[:, 0:tmpcols])

    for ct in range(CT):
        conv_copy(Wx[ct * 128:(ct + 1) * 128, :], s_wx[:, ct * 96:(ct + 1) * 96], 96)
    P.wait("sync", cV)
    dmaS(s_pr[0:DTR, 0:256], WdtS)
    P.wait("vector", qS)
    vcp(s_wdt[:], s_pr[0:DTR, 0:256])
    for i in range(4 * CT):
        conv_copy(convd[i], s_diag[:, i * 128:(i + 1) * 128], 128)

    def to_scratch(src, scr, nchunk, ccols):
        for j in range(nchunk):
            for p in range(0, ccols, 512):
                P.wait("sync", cV)
                dmaS(s_pr[:, 0:512], src[j * 128:(j + 1) * 128, p:p + 512])
                P.wait("vector", qS)
                P.wait("vector", qG)
                vcp(s_wpF[:, 0:512], s_pr[:, 0:512])
                P.wait("gpsimd", cV)
                dmaG(scr.ap()[j][:, p:p + 512], s_wpF[:, 0:512])

    to_scratch(Win, winS, DCH, 4096)
    to_scratch(Wout, woutS, CT, D)
    to_scratch(WrT, wrS, DCH, D)
    to_scratch(WkT, wkS, DCH, D)
    to_scratch(WvT, wvS, DCH, D)
    scr_done = qG.n

    P.wait("scalar", cV)
    P.wait("scalar", qG)
    act(s_silc[:, :], s_kc[:, O_CV:O_CV + DCH], AF.Silu)
    silc = cA.n
    for piece in range(2):
        for j in range(DCH):
            P.wait("sync", cP)
            dmaS(s_pr[:, 512:512 + 384],
                 wmodS[j * 128:(j + 1) * 128, piece * 384:(piece + 1) * 384])
            P.wait("tensor", qS)
            P.wait("tensor", cA, silc)
            mm(psA[0:1, 0:384], s_silc[:, j:j + 1],
               s_pr[:, 512:512 + 384], j == 0, j == DCH - 1)
        P.wait("scalar", cP)
        acp(s_row[0:1, piece * 384:piece * 384 + 384], psA[0:1, 0:384])
        P.wait("tensor", cA)
    P.wait("gpsimd", cA)
    dmaG(agmi.ap().unsqueeze(0), s_row[0:1, 0:768])
    P.wait("gpsimd", qG)
    coll("AllGather", agmi.ap().opt(), agmo.ap().opt())
    P.wait("sync", cc)
    cols_from_hbm(agmo.ap(), 48, s_mod[:, 0:48])
    P.wait("vector", cA)
    P.wait("scalar", cA)
    vts(s_mod[:, M_CA1:M_CA1 + DCH], s_mod[:, M_CA1:M_CA1 + DCH], 1.0, op0=OP.add)
    vts(s_mod[:, M_CB1:M_CB1 + DCH], s_mod[:, M_CB1:M_CB1 + DCH], 1.0, op0=OP.add)
    for col in (M_GA, M_GB):
        vts(s_pr[:, 0:DCH], s_mod[:, col:col + DCH], 0.0, op0=OP.is_ge)
        vts(s_pr[:, 0:DCH], s_pr[:, 0:DCH], 2.0, op0=OP.mult)
        vts(s_pr[:, 0:DCH], s_pr[:, 0:DCH], -1.0, op0=OP.add)
        P.wait("scalar", cV)
        act(s_pr[:, 8:8 + DCH], s_mod[:, col:col + DCH], AF.Abs)
        P.wait("vector", cA)
        vts(s_pr[:, 8:8 + DCH], s_pr[:, 8:8 + DCH], 0.25, op0=OP.max)
        vtt(s_mod[:, col:col + DCH], s_pr[:, 0:DCH], s_pr[:, 8:8 + DCH])
        if DBG and col == M_GA:
            P.wait("gpsimd", cV)
            dmaG(dbg["spr"], s_pr[:, 0:24])

    vcp(s_bc[:, B_SA:B_SA + DCH], s_mod[:, M_SA:M_SA + DCH])
    vcp(s_bc[:, B_SB:B_SB + DCH], s_mod[:, M_SB:M_SB + DCH])
    vtt(s_pr[:, 16:16 + DCH], s_mu1[:, 0:DCH], s_mod[:, M_SB:M_SB + DCH])
    vcp(s_bc[:, B_MBR:B_MBR + DCH], s_pr[:, 16:16 + DCH])
    vtt(s_pr[:, 16:16 + DCH], s_mu1[:, 8:8 + DCH], s_mod[:, M_SB:M_SB + DCH])
    vcp(s_bc[:, B_MBK:B_MBK + DCH], s_pr[:, 16:16 + DCH])
    cols_rdy = cV.n

    def matvec_scratch(scol, scr, kchunks, outcols, dst, slot):
        for mp in range(outcols // 512):
            for j in range(kchunks):
                P.wait("sync", cP)
                dmaS(s_wpB[:, 0:512], scr.ap()[j][:, mp * 512:(mp + 1) * 512])
                P.wait("tensor", qS)
                P.wait("tensor", cV, cols_rdy)
                ldw(s_bc[:, scol + j:scol + j + 1])
                mm(psB[0:1, :], s_bc[:, scol + j:scol + j + 1], s_wpB[:, 0:512],
                   j == 0, j == kchunks - 1)
            P.wait("scalar", cP)
            P.wait("scalar", qG)
            acp(s_row[0:1, 1024:1024 + 512], psB[0:1, :])
            P.wait("tensor", cA)
            P.wait("gpsimd", cA)
            dmaG(vecS.ap()[slot][mp * 512:(mp + 1) * 512].unsqueeze(0),
                 s_row[0:1, 1024:1024 + 512])
        P.wait("sync", qG)
        cols_from_hbm(vecS.ap()[slot][0:outcols], outcols // 128, dst)

    P.wait("sync", qG, scr_done)
    matvec_scratch(B_SA, winS, DCH, 4096, s_bxz[:], 0)
    matvec_scratch(B_SB, wrS, DCH, D, s_br[:], 1)
    matvec_scratch(B_SB, wkS, DCH, D, s_bk[:], 2)
    matvec_scratch(B_MBR, wrS, DCH, D, s_cr[:], 3)
    matvec_scratch(B_MBK, wkS, DCH, D, s_ck[:], 4)
    P.wait("vector", cA)
    for t in (s_cr, s_ck):
        vts(t[:], t[:], -1.0)
        P.op("vector", lambda e, t=t: e.tensor_scalar(
            out=t[:], in0=t[:], scalar1=s_kc[:, O_MSK:O_MSK + 1], scalar2=None,
            op0=OP.mult), cV)

    if DBG:
        P.wait("gpsimd", cV)
        P.wait("gpsimd", cA)
        dmaG(dbg["mod"], s_mod[:, :])
        dmaG(dbg["row"], s_row[0:1, :])
        dmaG(dbg["agmo"], agmo.ap())
        dmaG(dbg["bc"], s_bc[:, :])

    rg_wpF = Ring(P, 2)
    rg_wpB = Ring(P, 2)
    rg_xw = Ring(P, 2)
    rg_sq = Ring(P, 2)
    rg_zx = Ring(P, 4)
    rg_dr = Ring(P, 2)
    rg_dl = [Ring(P, 2), Ring(P, 2)]
    rg_xcu = [Ring(P, 1), Ring(P, 1)]
    rg_u = [Ring(P, 1), Ring(P, 1)]
    rg_bc = Ring(P, 2)
    rg_da = Ring(P, 2)
    rg_hs = Ring(P, 4)
    rg_tmp = Ring(P, 4)
    rg_yev = Ring(P, 1)
    rg_be = Ring(P, 2)
    rg_t1 = Ring(P, 2)
    rg_x2 = Ring(P, 2)

    cc_agx = [0] * ROUNDS
    cc_y = [0] * ROUNDS
    cc_agl = [0] * ROUNDS
    last_tts = [0]

    def front_end(k, fesub=99):
        t0 = k * SLAB
        P.phase_barrier()
        P.dma_guard = GUARD_FE
        P.wait("gpsimd", cP)
        P.wait("gpsimd", cV)
        P.wait("sync", cV)
        P.wait("sync", cA)

        for j in range(DCH):
            sl = rg_xw.acquire("sync")
            xwv = s_xw[:, sl * WIN:(sl + 1) * WIN]
            dmaS(xwv[:, 3:WIN], xT[j * 128:(j + 1) * 128, t0:t0 + SLAB])
            dmaS(xwv[:, 0:3],
                 xT[j * 128:(j + 1) * 128, XCOLS + k * 3:XCOLS + (k + 1) * 3])
            P.wait("gpsimd", qS)
            xbv = s_xb[:, j * WIN:(j + 1) * WIN]
            gcp(xbv, xwv)
            rg_xw.mark(cG)
            sq = rg_sq.acquire("gpsimd")
            sqv = s_sq[:, sq * WIN:(sq + 1) * WIN]
            gtt(sqv, xwv, xwv)
            rg_xw.mark(cG)
            P.wait("tensor", cG)
            ldw(s_one[:])
            mm(psD[0:1, 0:512], s_one[:], xbv[:, 3:WIN], j == 0, j == DCH - 1)
            mm(psB[0:1, 0:3], s_one[:], xbv[:, 0:3], j == 0, j == DCH - 1)
            mm(psC[0:1, 0:512], s_one[:], sqv[:, 3:WIN], j == 0, j == DCH - 1)
            mm(psB[0:1, 8:11], s_one[:], sqv[:, 0:3], j == 0, j == DCH - 1)
            rg_sq.mark(cP)
        P.wait("vector", cP)
        vts(s_row[0:1, R_M + 3:R_M + WIN], psD[0:1, 0:512], 1.0 / D)
        vts(s_row[0:1, R_M:R_M + 3], psB[0:1, 0:3], 1.0 / D)
        vts(s_row[0:1, R_S + 3:R_S + WIN], psC[0:1, 0:512], 1.0 / D)
        vts(s_row[0:1, R_S:R_S + 3], psB[0:1, 8:11], 1.0 / D)
        vtt(s_row[0:1, R_V:R_V + WIN], s_row[0:1, R_M:R_M + WIN],
            s_row[0:1, R_M:R_M + WIN])
        vtt(s_row[0:1, R_V:R_V + WIN], s_row[0:1, R_S:R_S + WIN],
            s_row[0:1, R_V:R_V + WIN], op=OP.subtract)
        P.wait("scalar", cV)
        act(s_row[0:1, R_I:R_I + WIN], s_row[0:1, R_V:R_V + WIN], AF.Ln, bias=EPS)
        act(s_row[0:1, R_I:R_I + WIN], s_row[0:1, R_I:R_I + WIN], AF.Exp, scale=-0.5)
        P.wait("gpsimd", cA)
        dmaG(rowS.ap()[0].unsqueeze(0), s_row[0:1, R_M:R_M + WIN])
        dmaG(rowS.ap()[1].unsqueeze(0), s_row[0:1, R_I:R_I + WIN])
        P.wait("sync", qG)
        dmaS(s_mrep[:], rowS.ap()[0].unsqueeze(0).broadcast_to((128, WIN)))
        dmaS(s_irep[:], rowS.ap()[1].unsqueeze(0).broadcast_to((128, WIN)))
        P.wait("vector", qS)
        for j in range(DCH):
            xbv = s_xb[:, j * WIN:(j + 1) * WIN]
            vtt(s_t1[:, 0:WIN], xbv, s_mrep[:], op=OP.subtract)
            vstt(s_h[:, j * WIN:(j + 1) * WIN], s_t1[:, 0:WIN],
                 s_mod[:, M_CA1 + j:M_CA1 + j + 1], s_irep[:], OP.mult, OP.mult)
        h_rdy = cV.n
        if DBG and k == 0:
            P.wait("gpsimd", cV, h_rdy)
            dmaG(dbg["h"], s_h[:, :])
        if fesub <= 1:
            return

        P.wait("sync", qG, scr_done)
        pslist = (psA, psB, psC)
        zi_ring = {}
        evac_prev = [0]
        for g in range(NG_WIN):
            mts = list(range(g * 3, min(MT, (g + 1) * 3)))
            w = len(mts) * 128
            for j in range(DCH):
                sl = rg_wpF.acquire("sync")
                dmaS(s_wpF[:, sl * 512:sl * 512 + w],
                     winS.ap()[j][:, mts[0] * 128:mts[0] * 128 + w])
                P.wait("tensor", qS)
                P.wait("tensor", cV, h_rdy)
                P.wait("tensor", cA, evac_prev[0])
                for mi, mt_ in enumerate(mts):
                    wp = s_wpF[:, sl * 512 + mi * 128:sl * 512 + (mi + 1) * 128]
                    ldw(wp)
                    mm(pslist[mi][:, :], wp, s_h[:, j * WIN + 3:j * WIN + WIN],
                       j == 0, j == DCH - 1)
                    if fesub >= 12 and mts[0] < CT:
                        hps = (psD, psY0, psY1)[mi]
                        mm(hps[:, 0:3], wp,
                           s_h[:, j * WIN:j * WIN + 3], j == 0, j == DCH - 1)
                rg_wpF.mark(cP)
            P.wait("scalar", cP)
            for mi, mt_ in enumerate(mts):
                if mt_ < CT:
                    if fesub >= 13:
                        act(s_xe[:, mt_ * WIN + 3:mt_ * WIN + WIN], pslist[mi][:, :],
                            AF.Identity, bias=s_bxz[:, mt_:mt_ + 1])
                    if fesub >= 14:
                        hps = (psD, psY0, psY1)[mi]
                        act(s_xe[:, mt_ * WIN:mt_ * WIN + 3], hps[:, 0:3],
                            AF.Identity, bias=s_bxz[:, mt_:mt_ + 1])
                else:
                    if fesub >= 15:
                        zi = mt_ - CT
                        zr = rg_zx.acquire("scalar")
                        act(s_zx[:, zr * SLAB:(zr + 1) * SLAB], pslist[mi][:, :],
                            AF.Identity, bias=s_bxz[:, mt_:mt_ + 1])
                        P.wait("gpsimd", cA)
                        dmaG(zp_st.ap()[k, zi], s_zx[:, zr * SLAB:(zr + 1) * SLAB])
                        rg_zx.mark(qG)
            evac_prev[0] = cA.n
        if fesub <= 19:
            return
        if fesub <= 2:
            return
        if k == 0:
            P.wait("vector", cA)
            for ct in range(CT):
                P.op("vector", lambda e, ct=ct: e.tensor_scalar(
                    out=s_xe[:, ct * WIN:ct * WIN + 3],
                    in0=s_xe[:, ct * WIN:ct * WIN + 3],
                    scalar1=s_kc[:, O_MSKH:O_MSKH + 1], scalar2=None,
                    op0=OP.mult), cV)

        for ct in range(CT):
            P.wait("tensor", cA)
            if k == 0:
                P.wait("tensor", cV)
            for kk in range(4):
                dg = s_diag[:, (kk * CT + ct) * 128:(kk * CT + ct + 1) * 128]
                ldw(dg)
                mm(psD[:, :], dg, s_xe[:, ct * WIN + kk:ct * WIN + kk + SLAB],
                   kk == 0, kk == 3)
            P.wait("scalar", cP)
            act(s_xe[:, ct * WIN + 3:(ct + 1) * WIN], psD[:, :], AF.Identity,
                bias=s_kc[:, O_CB + ct:O_CB + ct + 1])
        if fesub <= 3:
            return
        for ct in range(CT):
            act(s_xe[:, ct * WIN + 3:(ct + 1) * WIN],
                s_xe[:, ct * WIN + 3:(ct + 1) * WIN], AF.Silu)
        if DBG and k == 0:
            P.wait("gpsimd", cA)
            dmaG(dbg["xc"], s_xe[:, :])
        P.wait("tensor", cA)
        for ct in range(CT):
            ldw(s_wx[:, ct * 96:(ct + 1) * 96])
            mm(psD[0:96, :], s_wx[:, ct * 96:(ct + 1) * 96],
               s_xe[:, ct * WIN + 3:(ct + 1) * WIN], ct == 0, ct == CT - 1)
        P.wait("scalar", cP)
        act(s_xdb[:, :], psD[0:96, :], AF.Copy)
        if fesub <= 4:
            return
        P.wait("gpsimd", cA)
        for ct in range(CT):
            dmaG(xcAi[k].ap()[ct // 2, ct % 2], s_xe[:, ct * WIN + 3:(ct + 1) * WIN])
            dmaG(xc_st.ap()[k, ct], s_xe[:, ct * WIN + 3:(ct + 1) * WIN])
        dmaG(agxi[k].ap(), s_xdb[:, :])
        if fesub <= 5:
            return
        P.wait("gpsimd", qG)
        coll("AllToAll", xcAi[k].ap().opt(), xcAo[k].ap().opt())
        coll("AllGather", agxi[k].ap().opt(), agxo[k].ap().opt())
        cc_agx[k] = cc.n

    NG_WIN = (MT + 2) // 3

    def scan(k):
        P.phase_barrier()
        P.dma_guard = GUARD_SC
        P.wait("sync", cc, cc_agx[k])
        P.wait("sync", cV)
        P.wait("sync", cA)
        yev_prev = [0]
        for u in range(UPR):
            sl = rg_dr.acquire("sync")
            dmaS(s_dr[:, sl * UNIT:sl * UNIT + 512], agxo[k].ap()[2 * u, 0:DTR, :])
            dmaS(s_dr[:, sl * UNIT + 512:(sl + 1) * UNIT],
                 agxo[k].ap()[2 * u + 1, 0:DTR, :])
            for cti in range(2):
                rg_xcu[cti].acquire("sync")
                dmaS(s_xcu[:, cti * UNIT:cti * UNIT + 512], xcAo[k].ap()[2 * u, cti])
                dmaS(s_xcu[:, cti * UNIT + 512:(cti + 1) * UNIT],
                     xcAo[k].ap()[2 * u + 1, cti])
            dru = qS.n
            P.wait("tensor", qS, dru)
            P.wait("tensor", cA)
            for cti in range(2):
                dlr = rg_dl[cti].acquire("scalar")
                dlv = (s_dl if cti == 0 else s_dlb)
                P.wait("tensor", cA)
                for hf in range(2):
                    ldw(s_wdt[:, cti * 128:(cti + 1) * 128])
                    mm((psA if hf == 0 else psB)[:, :],
                       s_wdt[:, cti * 128:(cti + 1) * 128],
                       s_dr[:, sl * UNIT + hf * 512:sl * UNIT + (hf + 1) * 512],
                       True, True)
                rg_dr.mark(cP)
                P.wait("scalar", cP)
                P.wait("scalar", cV, last_tts[0])
                for hf in range(2):
                    pp = psA if hf == 0 else psB
                    act(s_da[:, hf * 512:(hf + 1) * 512] if cti == 0 else
                        s_da[:, UNIT + hf * 512:UNIT + (hf + 1) * 512],
                        pp[:, :], AF.Exp, bias=s_kc[:, O_BDT + cti:O_BDT + cti + 1])
                act(dlv[:, dlr * UNIT:(dlr + 1) * UNIT],
                    s_da[:, 0:UNIT] if cti == 0 else s_da[:, UNIT:2 * UNIT],
                    AF.Ln, bias=1.0)
                P.wait("vector", cA)
                P.wait("vector", qS, dru)
                rg_u[cti].acquire("vector")
                vtt(s_u[:, cti * UNIT:(cti + 1) * UNIT],
                    dlv[:, dlr * UNIT:(dlr + 1) * UNIT],
                    s_xcu[:, cti * UNIT:(cti + 1) * UNIT])
                rg_xcu[cti].mark(cV)
                rg_dl[cti].mark(cV)
            uvn = cV.n

            for n in range(NST):
                bs = rg_bc.acquire("sync")
                for hf in range(2):
                    src = agxo[k].ap()[2 * u + hf]
                    dmaS(s_brp[:, bs * UNIT + hf * 512:bs * UNIT + (hf + 1) * 512],
                         src[64 + n].unsqueeze(0).broadcast_to((128, 512)))
                    dmaS(s_crp[:, bs * UNIT + hf * 512:bs * UNIT + (hf + 1) * 512],
                         src[80 + n].unsqueeze(0).broadcast_to((128, 512)))
                bcn = qS.n
                for cti in range(2):
                    dlr = (rg_dl[cti].i - 1) % 2
                    dlv = (s_dl if cti == 0 else s_dlb)
                    da = rg_da.acquire("scalar")
                    act(s_da[:, da * UNIT:(da + 1) * UNIT],
                        dlv[:, dlr * UNIT:(dlr + 1) * UNIT], AF.Exp,
                        scale=s_asc[:, cti * NST + n:cti * NST + n + 1])
                    rg_dl[cti].mark(cA)
                    da_n = cA.n
                    P.wait("vector", qS, bcn)
                    P.wait("vector", cV, uvn)
                    vtt(s_d2[:, da * UNIT:(da + 1) * UNIT],
                        s_u[:, cti * UNIT:(cti + 1) * UNIT],
                        s_brp[:, bs * UNIT:(bs + 1) * UNIT])
                    rg_u[cti].mark(cV)
                    rg_bc.mark(cV)
                    hsl = rg_hs.acquire("vector")
                    P.wait("vector", cA, da_n)
                    first_u = (k == 0 and u == 0)
                    init = 0.0 if first_u else s_car[:, cti * NST + n:cti * NST + n + 1]
                    P.op("vector", lambda e, hsl=hsl, da=da, init=init:
                         e.tensor_tensor_scan(
                             out=s_hs[:, hsl * UNIT:(hsl + 1) * UNIT],
                             data0=s_da[:, da * UNIT:(da + 1) * UNIT],
                             data1=s_d2[:, da * UNIT:(da + 1) * UNIT],
                             initial=init, op0=OP.mult, op1=OP.add), cV)
                    rg_da.mark(cV)
                    tts_n = cV.n
                    P.wait("scalar", cV, tts_n)
                    acp(s_car[:, cti * NST + n:cti * NST + n + 1],
                        s_hs[:, (hsl + 1) * UNIT - 1:(hsl + 1) * UNIT])
                    rg_hs.mark(cA)
                    tsl = rg_tmp.acquire("gpsimd")
                    P.wait("gpsimd", cV, tts_n)
                    gtt(s_tmp[:, tsl * UNIT:(tsl + 1) * UNIT],
                        s_hs[:, hsl * UNIT:(hsl + 1) * UNIT],
                        s_crp[:, bs * UNIT:(bs + 1) * UNIT])
                    rg_hs.mark(cG)
                    rg_bc.mark(cG)
                    ym_n = cG.n
                    P.wait("tensor", cG, ym_n)
                    if n == 0:
                        P.wait("tensor", cA, yev_prev[0])
                    psY = psY0 if cti == 0 else psY1
                    ldw(s_id[:])
                    mm(psY[:, 0:512], s_id[:],
                       s_tmp[:, tsl * UNIT:tsl * UNIT + 512], n == 0, n == NST - 1)
                    mm(psY[:, 512:UNIT], s_id[:],
                       s_tmp[:, tsl * UNIT + 512:(tsl + 1) * UNIT], n == 0,
                       n == NST - 1)
                    rg_tmp.mark(cP)
            last_tts[0] = cV.n
            rg_yev.acquire("scalar")
            P.wait("scalar", cP)
            for cti in range(2):
                psY = psY0 if cti == 0 else psY1
                act(s_yev[:, cti * UNIT:(cti + 1) * UNIT], psY[:, :], AF.Copy)
            yev_prev[0] = cA.n
            if DBG and k == 0 and u == 0:
                P.wait("gpsimd", cA)
                dmaG(dbg["yev"], s_yev[:, :])
                dmaG(dbg["dl"], s_dl[:, :])
            P.wait("gpsimd", cA)
            for cti in range(2):
                for hf in range(2):
                    dmaG(yAi[k].ap()[2 * u + hf, cti],
                         s_yev[:, cti * UNIT + hf * 512:cti * UNIT + (hf + 1) * 512])
            rg_yev.mark(qG)
        P.wait("gpsimd", qG)
        coll("AllToAll", yAi[k].ap().opt(), yAo[k].ap().opt())
        cc_y[k] = cc.n

    def back_end(k):
        first = (k == 0)
        P.phase_barrier()
        P.dma_guard = GUARD_BE
        P.wait("sync", cc, cc_y[k])
        P.wait("sync", cV)
        P.wait("gpsimd", cP)
        ytiles = []
        for ct in range(CT):
            if ct < 8:
                dst = s_rr[:, ct * SLAB:(ct + 1) * SLAB]
            else:
                dst = s_ka[:, (ct - 8) * SLAB:(ct - 7) * SLAB]
            ytiles.append(dst)
            rg = rg_be.acquire("sync")
            dmaS(s_ybe[:, rg * SLAB:(rg + 1) * SLAB], yAo[k].ap()[ct // 2, ct % 2])
            dmaS(s_xcr[:, rg * SLAB:(rg + 1) * SLAB], xc_st.ap()[k, ct])
            dmaS(s_szr[:, rg * SLAB:(rg + 1) * SLAB], zp_st.ap()[k, ct])
            yld = qS.n
            if DBG and k == 0 and ct < 2:
                P.wait("gpsimd", qS, yld)
                dmaG(dbg["ybe"][:, ct * SLAB:(ct + 1) * SLAB],
                     s_ybe[:, rg * SLAB:(rg + 1) * SLAB])
            P.wait("scalar", qS, yld)
            act(s_szr[:, rg * SLAB:(rg + 1) * SLAB],
                s_szr[:, rg * SLAB:(rg + 1) * SLAB], AF.Silu)
            sz_n = cA.n
            P.wait("gpsimd", qS, yld)
            P.wait("gpsimd", cA, sz_n)
            tmpv = s_tmp[:, 0:SLAB] if rg == 0 else s_tmp[:, UNIT:UNIT + SLAB]
            P.op("gpsimd", lambda e, tmpv=tmpv, rg=rg, ct=ct: e.tensor_scalar(
                out=tmpv, in0=s_xcr[:, rg * SLAB:(rg + 1) * SLAB],
                scalar1=s_kc[:, O_DV + ct:O_DV + ct + 1], scalar2=None,
                op0=OP.mult), cG)
            gtt(tmpv, tmpv, s_ybe[:, rg * SLAB:(rg + 1) * SLAB], op=OP.add)
            gtt(dst, tmpv, s_szr[:, rg * SLAB:(rg + 1) * SLAB])
            rg_be.mark(cG)
        yt_all = cG.n
        if DBG and k == 0:
            P.wait("gpsimd", cG, yt_all)
            dmaG(dbg["yt"][0], s_rr[:, :])
            dmaG(dbg["yt"][1], s_ka[:, :])
        P.wait("tensor", cG, yt_all)
        xm_prev = [0]
        for g in range(3):
            mts = list(range(g * 3, min(DCH, (g + 1) * 3)))
            w = len(mts) * 128
            for ct in range(CT):
                sl = rg_wpB.acquire("sync")
                dmaS(s_wpB[:, sl * 512:sl * 512 + w],
                     woutS.ap()[ct][:, mts[0] * 128:mts[0] * 128 + w])
                P.wait("tensor", qS)
                P.wait("tensor", cV, xm_prev[0])
                for mi, mt_ in enumerate(mts):
                    wp = s_wpB[:, sl * 512 + mi * 128:sl * 512 + (mi + 1) * 128]
                    ldw(wp)
                    mm((psA, psB, psC)[mi][:, :], wp, ytiles[ct],
                       ct == 0, ct == CT - 1)
                rg_wpB.mark(cP)
            for mi, mt_ in enumerate(mts):
                x2s = rg_x2.acquire("sync")
                dmaS(s_x2[:, x2s * SLAB:(x2s + 1) * SLAB],
                     xT[mt_ * 128:(mt_ + 1) * 128, k * SLAB:(k + 1) * SLAB])
                P.wait("vector", qS)
                P.wait("vector", cP)
                vstt(s_xm[:, mt_ * SLAB:(mt_ + 1) * SLAB],
                     (psA, psB, psC)[mi][:, :],
                     s_mod[:, M_GA + mt_:M_GA + mt_ + 1],
                     s_x2[:, x2s * SLAB:(x2s + 1) * SLAB],
                     OP.mult, OP.add)
                rg_x2.mark(cV)
            xm_prev[0] = cV.n
        xm_n = cV.n
        if DBG and k == 0:
            P.wait("gpsimd", cV, xm_n)
            dmaG(dbg["xm"], s_xm[:, :])
        P.wait("gpsimd", cV, xm_n)
        for j in range(DCH):
            xmv = s_xm[:, j * SLAB:(j + 1) * SLAB]
            xbv = s_xb[:, j * WIN:j * WIN + SLAB]
            gcp(xbv, xmv)
            sq = rg_sq.acquire("gpsimd")
            sqv = s_sq[:, sq * WIN:sq * WIN + SLAB]
            gtt(sqv, xmv, xmv)
            P.wait("tensor", cG)
            ldw(s_one[:])
            mm(psD[0:1, 0:512], s_one[:], xbv, j == 0, j == DCH - 1)
            mm(psC[0:1, 0:512], s_one[:], sqv, j == 0, j == DCH - 1)
            rg_sq.mark(cP)
        P.wait("vector", cP)
        vts(s_row[0:1, R_M:R_M + 512], psD[0:1, 0:512], 1.0 / D)
        vts(s_row[0:1, R_S:R_S + 512], psC[0:1, 0:512], 1.0 / D)
        vtt(s_row[0:1, R_V:R_V + 512], s_row[0:1, R_M:R_M + 512],
            s_row[0:1, R_M:R_M + 512])
        vtt(s_row[0:1, R_V:R_V + 512], s_row[0:1, R_S:R_S + 512],
            s_row[0:1, R_V:R_V + 512], op=OP.subtract)
        P.wait("scalar", cV)
        act(s_row[0:1, R_I:R_I + 512], s_row[0:1, R_V:R_V + 512], AF.Ln, bias=EPS)
        act(s_row[0:1, R_I:R_I + 512], s_row[0:1, R_I:R_I + 512], AF.Exp, scale=-0.5)
        P.wait("gpsimd", cA)
        dmaG(rowS.ap()[2][0:512].unsqueeze(0), s_row[0:1, R_M:R_M + 512])
        dmaG(rowS.ap()[3][0:512].unsqueeze(0), s_row[0:1, R_I:R_I + 512])
        P.wait("sync", qG)
        dmaS(s_mrep[:, 0:512], rowS.ap()[2][0:512].unsqueeze(0).broadcast_to((128, 512)))
        dmaS(s_irep[:, 0:512], rowS.ap()[3][0:512].unsqueeze(0).broadcast_to((128, 512)))
        P.wait("vector", qS)
        for j in range(DCH):
            vtt(s_t1[:, 0:SLAB], s_xm[:, j * SLAB:(j + 1) * SLAB], s_mrep[:, 0:512],
                op=OP.subtract)
            vstt(s_l2[:, j * (SLAB + 1) + 1:(j + 1) * (SLAB + 1)], s_t1[:, 0:SLAB],
                 s_mod[:, M_CB1 + j:M_CB1 + j + 1], s_irep[:, 0:512],
                 OP.mult, OP.mult)
        l2_n = cV.n
        for j in range(DCH):
            vcp(s_lc[:, j:j + 1],
                s_l2[:, (j + 1) * (SLAB + 1) - 1:(j + 1) * (SLAB + 1)])
        P.wait("gpsimd", cV)
        dmaG(agli[k].ap().rearrange("(p j) -> p j", p=128), s_lc[:, 0:DCH])
        P.wait("gpsimd", qG)
        coll("AllGather", agli[k].ap().opt(), aglo[k].ap().opt())
        cc_agl[k] = cc.n
        P.wait("gpsimd", cc, cc_agl[k])

        def prevcol(e, k=k):
            rank = e.cc_rank([list(range(NCORE))])
            for r in e.Switch(rank, NCORE):
                if r == 0:
                    if k == 0:
                        e.memset(s_lc[:, 0:DCH], 0.0).then_inc(qG.sem, 16)
                    else:
                        e.dma_start(out=s_lc[:, 0:DCH],
                                    in_=aglo[k - 1].ap()[NCORE - 1].rearrange(
                                        "(p j) -> p j", p=128)).then_inc(qG.sem, 16)
                else:
                    e.dma_start(out=s_lc[:, 0:DCH],
                                in_=aglo[k].ap()[r - 1].rearrange(
                                    "(p j) -> p j", p=128)).then_inc(qG.sem, 16)
        P.raw("gpsimd", prevcol)
        qG.n += 16
        P.wait("vector", qG)
        for j in range(DCH):
            vcp(s_l2[:, j * (SLAB + 1):j * (SLAB + 1) + 1], s_lc[:, j:j + 1])
        l2c_n = cV.n

        P.wait("vector", cV, l2c_n)
        for j in range(DCH):
            l2v = s_l2[:, j * (SLAB + 1):(j + 1) * (SLAB + 1)]
            vts(s_t1[:, 0:SLAB], l2v[:, 1:SLAB + 1], s_kc[:, O_MUR + j:O_MUR + j + 1])
            P.op("vector", lambda e, j=j, l2v=l2v: e.scalar_tensor_tensor(
                out=s_xb[:, j * WIN:j * WIN + SLAB], in0=l2v[:, 0:SLAB],
                scalar=s_mu1[:, 0 + j:0 + j + 1], in1=s_t1[:, 0:SLAB],
                op0=OP.mult, op1=OP.add), cV)
            vts(s_t1[:, SLAB:2 * SLAB], l2v[:, 1:SLAB + 1],
                s_kc[:, O_MUK + j:O_MUK + j + 1])
            P.op("vector", lambda e, j=j, l2v=l2v: e.scalar_tensor_tensor(
                out=s_h[:, j * WIN:j * WIN + SLAB], in0=l2v[:, 0:SLAB],
                scalar=s_mu1[:, 8 + j:8 + j + 1], in1=s_t1[:, SLAB:2 * SLAB],
                op0=OP.mult, op1=OP.add), cV)
        xrk_n = cV.n
        if DBG and k == 0:
            P.wait("gpsimd", cV, xrk_n)
            dmaG(dbg["l2"], s_l2[:, :])
        P.wait("tensor", cV, xrk_n)
        rk_prev = [0]
        for which, (wscr, bias, corr, xsrc, dst) in enumerate((
                (wrS, s_br, s_cr, s_xb, s_rr),
                (wkS, s_bk, s_ck, s_h, s_ka))):
            for g in range(3):
                mts = list(range(g * 3, min(DCH, (g + 1) * 3)))
                w = len(mts) * 128
                for j in range(DCH):
                    sl = rg_wpB.acquire("sync")
                    dmaS(s_wpB[:, sl * 512:sl * 512 + w],
                         wscr.ap()[j][:, mts[0] * 128:mts[0] * 128 + w])
                    P.wait("tensor", qS)
                    P.wait("tensor", cA, rk_prev[0])
                    for mi, mt_ in enumerate(mts):
                        wp = s_wpB[:, sl * 512 + mi * 128:sl * 512 + (mi + 1) * 128]
                        ldw(wp)
                        mm((psA, psB, psC)[mi][:, :], wp,
                           xsrc[:, j * WIN:j * WIN + SLAB], j == 0, j == DCH - 1)
                    rg_wpB.mark(cP)
                P.wait("vector", cP)
                for mi, mt_ in enumerate(mts):
                    pp = (psA, psB, psC)[mi]
                    P.op("vector", lambda e, pp=pp, bias=bias, mt_=mt_:
                         e.tensor_scalar(out=pp[:, :], in0=pp[:, :],
                                         scalar1=bias[:, mt_:mt_ + 1],
                                         scalar2=None, op0=OP.add), cV)
                    if first:
                        P.op("vector", lambda e, pp=pp, corr=corr, mt_=mt_:
                             e.tensor_tensor(out=pp[:, 0:1], in0=pp[:, 0:1],
                                             in1=corr[:, mt_:mt_ + 1], op=OP.add), cV)
                P.wait("scalar", cV)
                for mi, mt_ in enumerate(mts):
                    pp = (psA, psB, psC)[mi]
                    if which == 0:
                        act(dst[:, mt_ * SLAB:(mt_ + 1) * SLAB], pp[:, :], AF.Sigmoid)
                    else:
                        act(s_da[:, 0:512], pp[:, :], AF.Relu)
                        act(dst[:, mt_ * SLAB:(mt_ + 1) * SLAB], s_da[:, 0:512],
                            AF.Square)
                rk_prev[0] = cA.n
        rk_n = cA.n
        if DBG and k == 0:
            P.wait("gpsimd", cA, rk_n)
            dmaG(dbg["rr"], s_rr[:, :])
            dmaG(dbg["ka"], s_ka[:, :])
        P.wait("tensor", cA, rk_n)
        fin_prev = [0]
        for g in range(3):
            mts = list(range(g * 3, min(DCH, (g + 1) * 3)))
            w = len(mts) * 128
            for j in range(DCH):
                sl = rg_wpB.acquire("sync")
                dmaS(s_wpB[:, sl * 512:sl * 512 + w],
                     wvS.ap()[j][:, mts[0] * 128:mts[0] * 128 + w])
                P.wait("tensor", qS)
                P.wait("tensor", cV, fin_prev[0])
                for mi, mt_ in enumerate(mts):
                    wp = s_wpB[:, sl * 512 + mi * 128:sl * 512 + (mi + 1) * 128]
                    ldw(wp)
                    mm((psA, psB, psC)[mi][:, :], wp, s_ka[:, j * SLAB:(j + 1) * SLAB],
                       j == 0, j == DCH - 1)
                rg_wpB.mark(cP)
            P.wait("vector", cP)
            for mi, mt_ in enumerate(mts):
                pp = (psA, psB, psC)[mi]
                t1s = rg_t1.acquire("vector")
                vtt(s_t1[:, t1s * SLAB:(t1s + 1) * SLAB],
                    s_rr[:, mt_ * SLAB:(mt_ + 1) * SLAB], pp[:, :])
                P.op("vector", lambda e, mt_=mt_, t1s=t1s: e.scalar_tensor_tensor(
                    out=s_t1[:, t1s * SLAB:(t1s + 1) * SLAB],
                    in0=s_t1[:, t1s * SLAB:(t1s + 1) * SLAB],
                    scalar=s_mod[:, M_GB + mt_:M_GB + mt_ + 1],
                    in1=s_xm[:, mt_ * SLAB:(mt_ + 1) * SLAB],
                    op0=OP.mult, op1=OP.add), cV)
                P.wait("sync", cV)
                dmaS(outT[mt_ * 128:(mt_ + 1) * 128, k * SLAB:(k + 1) * SLAB],
                     s_t1[:, t1s * SLAB:(t1s + 1) * SLAB])
                rg_t1.mark(qS)
                fin_prev[0] = cV.n

    stage = 1
    for k in range(ROUNDS):
        if stage >= nstage:
            break
        front_end(k, fesub)
        stage += 1
        if stage >= nstage:
            break
        scan(k)
        stage += 1
        if stage >= nstage:
            break
        back_end(k)
        stage += 1

    P.wait("sync", qS)
    P.wait("sync", qG)
    P.wait("sync", cc)
    P.wait("gpsimd", qS)
    P.wait("gpsimd", qG)

    with nc.Block() as block:
        @block.sync
        def _(eng):
            P.emit("sync", eng)

        @block.tensor
        def _(eng):
            P.emit("tensor", eng)

        @block.vector
        def _(eng):
            P.emit("vector", eng)

        @block.scalar
        def _(eng):
            P.emit("scalar", eng)

        @block.gpsimd
        def _(eng):
            P.emit("gpsimd", eng)

    for t in reversed(ctx):
        t.__exit__(None, None, None)
    return nc


_cache = {}


def make_in_maps(L0_attn_x, L0_attn_W_in, L0_attn_conv_w, L0_attn_conv_b,
                 L0_attn_W_x, L0_attn_W_dt, L0_attn_b_dt, L0_attn_A_log,
                 L0_attn_D, L0_attn_W_out, L0_ffn_W_r, L0_ffn_W_k, L0_ffn_W_v,
                 L0_ffn_mu_r, L0_ffn_mu_k, L0_c, L0_wmod):
    x = np.ascontiguousarray(np.asarray(L0_attn_x, np.float32).reshape(TOK, D).T)

    cw = np.asarray(L0_attn_conv_w, np.float32).reshape(CI, 4)
    convd = np.zeros((4 * CT, 128, 128), np.float32)
    for kk in range(4):
        for ct in range(CT):
            d = cw[ct * 128:(ct + 1) * 128, kk]
            convd[kk * CT + ct][np.arange(128), np.arange(128)] = d

    ident = np.eye(128, dtype=np.float32)
    ones = np.ones((128, 1), np.float32)

    def cols(v, n):
        return np.asarray(v, np.float32).reshape(n, 128).T

    in_maps = []
    for r in range(NCORE):
        xr = np.zeros((D, XHALO), np.float32)
        for k in range(ROUNDS):
            t0 = k * RTOK + r * SLAB
            xr[:, k * SLAB:(k + 1) * SLAB] = x[:, t0:t0 + SLAB]
            if t0 > 0:
                xr[:, XCOLS + k * 3:XCOLS + (k + 1) * 3] = x[:, t0 - 3:t0]

        mask0 = np.full((128, 1), 1.0 if r == 0 else 0.0, np.float32)
        maskh = np.full((128, 1), 0.0 if r == 0 else 1.0, np.float32)
        KCm = np.concatenate([
            cols(np.asarray(L0_c, np.float32).reshape(D), DCH),
            cols(L0_ffn_mu_r, DCH), cols(L0_ffn_mu_k, DCH),
            cols(L0_attn_conv_b, CT), cols(L0_attn_D, CT),
            cols(np.asarray(L0_attn_b_dt, np.float32)[r * 256:(r + 1) * 256], 2),
            np.asarray(L0_attn_A_log, np.float32)[r * 256:(r + 1) * 256, :].reshape(
                2, 128, NST).transpose(1, 0, 2).reshape(128, 32),
            ones, mask0, maskh, ident,
            np.full((128, 1), EPS, np.float32),
            np.full((128, 1), 1.0, np.float32),
            np.full((128, 1), -0.5, np.float32)], axis=1).astype(np.float32)

        in_maps.append({
            "xT": np.ascontiguousarray(xr),
            "Win": np.asarray(L0_attn_W_in, np.float32),
            "convd": convd,
            "Wx": np.asarray(L0_attn_W_x, np.float32),
            "WdtS": np.ascontiguousarray(
                np.asarray(L0_attn_W_dt, np.float32)[:, r * 256:(r + 1) * 256]),
            "Wout": np.asarray(L0_attn_W_out, np.float32),
            "WrT": np.ascontiguousarray(np.asarray(L0_ffn_W_r, np.float32).T),
            "WkT": np.ascontiguousarray(np.asarray(L0_ffn_W_k, np.float32).T),
            "WvT": np.ascontiguousarray(np.asarray(L0_ffn_W_v, np.float32).T),
            "wmodS": np.ascontiguousarray(
                np.asarray(L0_wmod, np.float32)[:, r * 768:(r + 1) * 768]),
            "KC": KCm,
        })
    return in_maps


def kernel(**inputs):
    in_maps = make_in_maps(**inputs)
    if "prog" not in _cache:
        _cache["prog"] = build_program()
    res = run_bass_kernel_spmd(_cache["prog"], in_maps, core_ids=list(range(NCORE)))

    out = np.zeros((TOK, D), np.float32)
    for r in range(NCORE):
        oT = res.results[r]["outT"]
        for k in range(ROUNDS):
            t0 = k * RTOK + r * SLAB
            out[t0:t0 + SLAB, :] = oT[:, k * SLAB:(k + 1) * SLAB].T
    return out.reshape(1, TOK, D)


